# version 04 (final)
# baseline (speedup 1.0000x reference)
import numpy as np
import concourse.bass as bass
import concourse.mybir as mybir
from concourse.bass_utils import run_bass_kernel_spmd

F32 = mybir.dt.float32
BF16 = mybir.dt.bfloat16
AF = mybir.ActivationFunctionType
ALU = mybir.AluOpType
AX = mybir.AxisListType

NCORE = 8
T = 8192
DM = 2048
NH = 32
HPC = NH // NCORE
HD = 64
CI = 128
CO = 512
NOC = T // CO
MPO = CO // CI
TPC = T // NCORE
SCALE = HD ** -0.5

ENGINES = ("PE", "DVE", "ACT", "POOL", "SP")
ENG_ATTR = {"PE": "tensor", "DVE": "vector", "ACT": "scalar", "POOL": "gpsimd", "SP": "sync"}
DMA_RING = 8


class Buf:
    __slots__ = ("h", "name", "shape", "space")

    def __init__(self, h, name, shape, space):
        self.h = h
        self.name = name
        self.shape = shape
        self.space = space

    def __getitem__(self, idx):
        ps, fs = idx
        if isinstance(ps, int):
            ps = slice(ps, ps + 1)
        if isinstance(fs, int):
            fs = slice(fs, fs + 1)
        p0, p1 = ps.start or 0, ps.stop if ps.stop is not None else self.shape[0]
        f0, f1 = fs.start or 0, fs.stop if fs.stop is not None else self.shape[1]
        return View(self, p0, p1, f0, f1)

    @property
    def full(self):
        return View(self, 0, self.shape[0], 0, self.shape[1] if len(self.shape) > 1 else 1)


class View:
    __slots__ = ("buf", "p0", "p1", "f0", "f1", "_ap")

    def __init__(self, buf, p0, p1, f0, f1, ap=None):
        self.buf = buf
        self.p0, self.p1, self.f0, self.f1 = p0, p1, f0, f1
        self._ap = ap

    @property
    def ap(self):
        if self._ap is not None:
            return self._ap
        return self.buf.h[self.p0:self.p1, self.f0:self.f1]

    def __repr__(self):
        return f"{self.buf.name}[{self.p0}:{self.p1},{self.f0}:{self.f1}]"


def dview(buf, ap):
    return View(buf, 0, 1 << 30, 0, 1 << 30, ap=ap)


class Op:
    __slots__ = ("engine", "fn", "deps", "name", "idx", "inc", "cum", "needed",
                 "dma", "waits", "sem_key", "target", "coll")

    def __init__(self, engine, fn, name, dma, coll):
        self.engine = engine
        self.fn = fn
        self.deps = set()
        self.name = name
        self.idx = -1
        self.inc = 0
        self.cum = 0
        self.needed = False
        self.dma = dma
        self.coll = coll
        self.waits = ()
        self.sem_key = None
        self.target = 0


class Prog:
    def __init__(self, nc):
        self.nc = nc
        self.ops = []
        self.streams = {e: [] for e in ENGINES}
        self.acc = {}

    def _track(self, o, views, is_write):
        for v in views:
            if v is None:
                continue
            key = v.buf.name
            psum = v.buf.space == 'P'
            f0, f1 = v.f0, v.f1
            if psum:
                f0 = (f0 // 512) * 512
                f1 = ((f1 + 511) // 512) * 512
            lst = self.acc.setdefault(key, [])
            keep = []
            for ent in lst:
                eop, ep0, ep1, ef0, ef1, ew = ent
                overlap = not (ep1 <= v.p0 or v.p1 <= ep0 or ef1 <= f0 or f1 <= ef0)
                if overlap and (is_write or ew):
                    o.deps.add(eop)
                if is_write and ep0 >= v.p0 and ep1 <= v.p1 and ef0 >= f0 and ef1 <= f1:
                    continue
                keep.append(ent)
            keep.append([o, v.p0, v.p1, f0, f1, is_write])
            self.acc[key] = keep

    def op(self, engine, fn, reads=(), writes=(), deps=(), name="", dma=False, coll=False):
        o = Op(engine, fn, name, dma, coll)
        for d in deps:
            if d is not None:
                o.deps.add(d)
        self._track(o, reads, False)
        self._track(o, writes, True)
        o.deps.discard(o)
        o.idx = len(self.streams[engine])
        self.streams[engine].append(o)
        self.ops.append(o)
        return o

    def dma(self, queue, dst, src, name="", **kw):
        return self.op(queue, lambda e: e.dma_start(out=dst.ap, in_=src.ap, **kw),
                       reads=[src], writes=[dst], name=name, dma=True)

    def barrier(self, name="bar"):
        tails = [self.streams[e][-1] for e in ENGINES if self.streams[e]]
        outs = []
        for e in ENGINES:
            o = self.op(e, lambda eng: eng.nop(), deps=list(tails), name=name + e)
            outs.append(o)
        for key in self.acc:
            self.acc[key] = [[outs[0], 0, 1 << 30, 0, 1 << 30, True]]
        return outs

    def emit(self):
        nc = self.nc
        for o in self.ops:
            for d in o.deps:
                d.needed = True
        self._ctxs = []

        def mksem(nm):
            cm = nc.semaphore(nm)
            self._ctxs.append(cm)
            return cm.__enter__()

        eng_sem = {}
        ring_sems = {}
        for e in ENGINES:
            if any((not o.dma and not o.coll) for o in self.streams[e]):
                eng_sem[e] = mksem(f"es{e}")
            nd = sum(1 for o in self.streams[e] if o.dma or o.coll)
            if nd:
                ring_sems[e] = [mksem(f"dr{e}{i}") for i in range(min(DMA_RING, nd))]
        issue_waits = {}
        for e in ENGINES:
            c = 0
            nd = 0
            nring = len(ring_sems.get(e, ()))
            for o in self.streams[e]:
                if o.dma or o.coll:
                    r = nd % nring
                    ep = nd // nring
                    nd += 1
                    o.sem_key = ("R", e, r)
                    o.inc = 1 if o.coll else 16
                    o.target = None
                    issue_waits[o] = (r, ep)
                else:
                    if o.needed:
                        o.inc = 1
                        c += 1
                    o.sem_key = ("E", e)
                    o.cum = c
                    o.target = c
        for e in ENGINES:
            nring = len(ring_sems.get(e, ()))
            if not nring:
                continue
            cums = [0] * nring
            for o in self.streams[e]:
                if o.dma or o.coll:
                    r, ep = issue_waits[o]
                    prev = cums[r]
                    cums[r] += o.inc
                    o.target = cums[r]
                    issue_waits[o] = (ring_sems[e][r], prev) if ep > 0 else None
        sem_of = {}
        for e in ENGINES:
            if e in eng_sem:
                sem_of[("E", e)] = eng_sem[e]
            for i, s in enumerate(ring_sems.get(e, ())):
                sem_of[("R", e, i)] = s
        for e in ENGINES:
            water = {}
            for o in self.streams[e]:
                waits = {}
                for d in o.deps:
                    k = d.sem_key
                    if k == ("E", e) and d.inc == 0:
                        continue
                    tgt = d.target
                    if tgt > water.get(k, 0):
                        waits[k] = max(waits.get(k, 0), tgt)
                o.waits = [(sem_of[k], v) for k, v in waits.items()]
                iw = issue_waits.get(o)
                if iw is not None:
                    sem, val = iw
                    if val > 0:
                        o.waits = list(o.waits) + [(sem, val)]
                for k, v in waits.items():
                    water[k] = v
        blk = nc.Block()
        blk.__enter__()
        for e in ENGINES:
            stream = self.streams[e]
            if not stream:
                continue

            def body(eng, stream=stream, e=e):
                for o in stream:
                    for (sem, val) in o.waits:
                        eng.wait_ge(sem, val)
                    inst = o.fn(eng)
                    if o.inc:
                        inst.then_inc(sem_of[o.sem_key], o.inc)

            getattr(blk, ENG_ATTR[e])(body)
        blk.__exit__(None, None, None)
        for cm in self._ctxs:
            cm.__exit__(None, None, None)
        nops = {e: len(self.streams[e]) for e in ENGINES}
        return nops


class Ctx:

    def __init__(self, p):
        self.p = p

    def mm(self, out, lhsT, rhs, start=True, stop=True, name="mm"):
        return self.p.op("PE", lambda e: e.matmul(out.ap, lhsT.ap, rhs.ap, start=start, stop=stop),
                         reads=[lhsT, rhs], writes=[out], name=name)

    def tr(self, out, in_, ident, name="tr"):
        return self.p.op("PE", lambda e: e.transpose(out.ap, in_.ap, ident.ap),
                         reads=[in_, ident], writes=[out], name=name)

    def cp(self, out, in_, eng="DVE", name="cp"):
        return self.p.op(eng, lambda e: e.tensor_copy(out.ap, in_.ap),
                         reads=[in_], writes=[out], name=name)

    def tt(self, out, a, b, op, name="tt"):
        return self.p.op("DVE", lambda e: e.tensor_tensor(out.ap, a.ap, b.ap, op),
                         reads=[a, b], writes=[out], name=name)

    def ts(self, out, a, s1, op0, s2=None, op1=ALU.bypass, name="ts"):
        s1a = s1.ap if isinstance(s1, View) else s1
        s2a = s2.ap if isinstance(s2, View) else s2
        rd = [a] + [x for x in (s1, s2) if isinstance(x, View)]
        return self.p.op("DVE", lambda e: e.tensor_scalar(out.ap, a.ap, s1a, s2a, op0, op1),
                         reads=rd, writes=[out], name=name)

    def stt(self, out, a, s, b, op0, op1, name="stt"):
        sa = s.ap if isinstance(s, View) else s
        rd = [a, b] + ([s] if isinstance(s, View) else [])
        return self.p.op("DVE", lambda e: e.scalar_tensor_tensor(out.ap, a.ap, sa, b.ap, op0, op1),
                         reads=rd, writes=[out], name=name)

    def red(self, out, in_, op, name="red", negate=None):
        return self.p.op("DVE", lambda e: e.tensor_reduce(out.ap, in_.ap, AX.X, op, negate=negate),
                         reads=[in_], writes=[out], name=name)

    def recip(self, out, in_, name="recip"):
        return self.p.op("DVE", lambda e: e.reciprocal(out.ap, in_.ap),
                         reads=[in_], writes=[out], name=name)

    def memset(self, out, val, eng="DVE", name="memset"):
        return self.p.op(eng, lambda e: e.memset(out.ap, val),
                         reads=[], writes=[out], name=name)

    def act(self, out, in_, func, bias=0.0, scale=1.0, name="act"):
        ba = bias.ap if isinstance(bias, View) else bias
        rd = [in_] + ([bias] if isinstance(bias, View) else [])
        return self.p.op("ACT", lambda e: e.activation(out.ap, in_.ap, func, bias=ba, scale=scale),
                         reads=rd, writes=[out], name=name)


def mkbuf(p, nc, ctxstack, name, shape, dtype, space='S'):
    if space == 'S':
        cm = nc.sbuf_tensor(name, list(shape), dtype)
    else:
        cm = nc.psum_tensor(name, list(shape), dtype)
    h = cm.__enter__()
    ctxstack.append(cm)
    return Buf(h, name, tuple(shape), space)


def dbuf(nc, name, shape, dtype, kind="Internal", addr_space="Local"):
    t = nc.dram_tensor(name, list(shape), dtype, kind=kind, addr_space=addr_space)
    return Buf(t.ap(), name, tuple(shape), 'D')



def _cst_cols():
    c = {}
    o = 0
    for nm, w in [("ident", 128), ("trilm1", 128), ("triu1", 128),
                  ("bdtril", 128), ("odtril", 128), ("triuIncl", 128),
                  ("cumIncl", 128), ("masknegT", 512), ("ones", 128),
                  ("hselA", 1), ("hselB", 1), ("eye2", 64),
                  ("c1e6", 1), ("c1e12", 1), ("masknegS", 128),
                  ("dmask", 8), ("exp8", 64), ("c1e5", 1)]:
        c[nm] = (o, o + w)
        o += w
    return c, o
CST, CST_W = _cst_cols()


def host_consts():
    cst = np.zeros((128, CST_W), np.float32)
    def put(nm, v):
        a, b = CST[nm]
        cst[:, a:b] = v
    ii = np.arange(128)
    tril1 = np.tril(np.ones((128, 128), np.float32), -1)
    bd = np.zeros((128, 128), np.float32)
    for j in range(0, 128, 16):
        bd[j:j+16, j:j+16] = 1.0
    put("ident", np.eye(128, dtype=np.float32))
    put("trilm1", tril1)
    put("triu1", np.triu(np.ones((128, 128), np.float32), 1))
    put("bdtril", tril1 * bd)
    put("odtril", tril1 * (1.0 - bd))
    put("triuIncl", (ii[:, None] <= ii[None, :]).astype(np.float32))
    put("cumIncl", (ii[:, None] <= ii[None, :]).astype(np.float32))
    put("masknegT", np.tile((ii[:, None] > ii[None, :]).astype(np.float32) * 1e30, (1, 4)))
    put("ones", 1.0)
    hA = np.zeros((128, 1), np.float32); hA[0:64] = 1.0
    hB = np.zeros((128, 1), np.float32); hB[64:128] = 1.0
    put("hselA", hA)
    put("hselB", hB)
    e2 = np.zeros((128, 64), np.float32)
    e2[0:64] = np.eye(64); e2[64:128] = np.eye(64)
    put("eye2", e2)
    put("c1e6", 1e-6)
    put("c1e12", 1e-12)
    put("masknegS", (ii[:, None] > ii[None, :]).astype(np.float32) * 1e30)
    dm = np.zeros((128, 8), np.float32)
    for n_ in range(8):
        dm[n_ * 8:(n_ + 1) * 8, n_] = 1.0
    put("dmask", dm)
    e8 = np.zeros((128, 64), np.float32)
    for n_ in range(8):
        e8[n_, n_ * 8:(n_ + 1) * 8] = 1.0
    put("exp8", e8)
    put("c1e5", 1e-5)
    return cst


def _c(bufs, nm, rows=slice(0, 128), width=None):
    a, b = CST[nm]
    if width is not None:
        b = a + width
    return bufs["cstS"][rows, a:b]


def emit_inner(c, bufs, h, oc, m, Tloc):
    p = c.p
    CO_ = CO
    hb = 64 * h
    t4 = m
    QbT = bufs["qnT"][hb:hb + 64, 128 * m:128 * m + 128]
    KbT = bufs["knT"][hb:hb + 64, 128 * m:128 * m + 128]
    WbT = bufs["wnT"][hb:hb + 64, 128 * m:128 * m + 128]
    bbc = bufs["betaS"][:, 2 * t4 + h:2 * t4 + h + 1]
    nbbc = bufs["nbetaS"][:, 2 * t4 + h:2 * t4 + h + 1]
    gnc = bufs["GnS"][:, 2 * t4 + h:2 * t4 + h + 1]
    loc = bufs["localS"]
    ploc = bufs[f"ploc{h}"]
    sloc = bufs[f"sloc{h}"]
    Qe = bufs[f"Qext{h}"]
    ppi, ppj, pp3 = bufs["ppi"], bufs["ppj"], bufs["pp3"]
    B = lambda nm: bufs[nm]
    idC = _c(bufs, "ident")
    c.mm(ppi[:, 0:128], WbT, WbT, name="iWW")
    c.cp(B("sWW")[:, 0:128], ppi[:, 0:128], name="iWWev")
    c.tt(B("sbd")[:, 0:128], B("sWW")[:, 0:128], _c(bufs, "bdtril"), ALU.mult, name="ibd")
    c.stt(B("sO")[:, 0:128], B("sWW")[:, 0:128], bbc, _c(bufs, "odtril"), ALU.mult, ALU.mult, name="iO")
    c.ts(B("sLbd")[:, 0:128], B("sbd")[:, 0:128], bbc, ALU.mult, name="iLbd")
    c.ts(B("sdb")[:, 0:128], idC, bbc, ALU.mult, name="idb")
    c.mm(ppj[:, 0:128], B("sbd")[:, 0:128], B("sdb")[:, 0:128], name="iLbdT")
    c.cp(B("sLbdT")[:, 0:128], ppj[:, 0:128], name="iLbdTev")
    c.mm(ppi[:, 128:256], B("sLbdT")[:, 0:128], B("sLbd")[:, 0:128], name="iL2")
    c.cp(B("sL2")[:, 0:128], ppi[:, 128:256], name="iL2ev")
    c.mm(ppj[:, 128:256], B("sLbd")[:, 0:128], B("sLbdT")[:, 0:128], name="iL2T")
    c.cp(B("sL2T")[:, 0:128], ppj[:, 128:256], name="iL2Tev")
    c.mm(ppi[:, 256:384], B("sL2T")[:, 0:128], B("sL2")[:, 0:128], name="iL4")
    c.cp(B("sL4")[:, 0:128], ppi[:, 256:384], name="iL4ev")
    c.mm(ppj[:, 256:384], B("sL2")[:, 0:128], B("sL2T")[:, 0:128], name="iL4T")
    c.cp(B("sL4T")[:, 0:128], ppj[:, 256:384], name="iL4Tev")
    c.mm(ppi[:, 384:512], B("sL4T")[:, 0:128], B("sL4")[:, 0:128], name="iL8")
    c.cp(B("sL8")[:, 0:128], ppi[:, 384:512], name="iL8ev")
    c.stt(B("sP")[:, 0:128], B("sLbd")[:, 0:128], -1.0, idC, ALU.mult, ALU.add, name="iP1")
    c.stt(B("sPT")[:, 0:128], B("sLbdT")[:, 0:128], -1.0, idC, ALU.mult, ALU.add, name="iP1T")
    for (Lk, tag) in (("sL2", "2"), ("sL4", "4"), ("sL8", "8")):
        c.mm(ppi[:, 0:128], B("sPT")[:, 0:128], B(Lk)[:, 0:128], name="iPm" + tag)
        c.mm(ppj[:, 0:128], B(Lk)[:, 0:128], B("sPT")[:, 0:128], name="iPTm" + tag)
        c.tt(B("sP")[:, 0:128], B("sP")[:, 0:128], ppi[:, 0:128], ALU.add, name="iPa" + tag)
        c.tt(B("sPT")[:, 0:128], B("sPT")[:, 0:128], ppj[:, 0:128], ALU.add, name="iPTa" + tag)
    c.mm(ppi[:, 0:128], B("sPT")[:, 0:128], B("sO")[:, 0:128], name="iN")
    c.ts(B("sN")[:, 0:128], ppi[:, 0:128], -1.0, ALU.mult, name="iNev")
    c.mm(ppj[:, 0:128], WbT, QbT, name="iRp")
    c.tt(B("sX")[:, 0:128], ppj[:, 0:128], _c(bufs, "triuIncl"), ALU.mult, name="iXa")
    c.tt(B("sX")[:, 128:256], B("sWW")[:, 0:128], _c(bufs, "triu1"), ALU.mult, name="iXb")
    src = "sX"
    for it in range(7):
        c.mm(ppi[:, 0:256], B("sN")[:, 0:128], B(src)[:, 0:256], name=f"iH{it}")
        c.tt(B("sB")[:, 0:256], B("sX")[:, 0:256], ppi[:, 0:256], ALU.add, name=f"iHa{it}")
        src = "sB"
    c.mm(ppi[:, 0:256], B("sP")[:, 0:128], B("sB")[:, 0:256], name="iY")
    c.cp(B("sY")[:, 0:256], ppi[:, 0:256], name="iYev")
    c.tr(ppj[:, 0:64], WbT, _c(bufs, "eye2", slice(hb, hb + 64), 64), name="iWtr")
    c.cp(B("sWtok")[:, 0:64], ppj[:, 0:64], name="iWtokev")
    c.ts(B("sWbb")[:, 0:64], B("sWtok")[:, 0:64], bbc, ALU.mult, name="iWbb")
    c.ts(B("sRb")[:, 0:128], B("sY")[:, 0:128], bbc, ALU.mult, name="iRb")
    c.mm(pp3[hb:hb + 64, 0:128], B("sWtok")[:, 0:64], B("sRb")[:, 0:128], name="iQt")
    c.tt(B("sQt")[hb:hb + 64, 0:128], QbT, pp3[hb:hb + 64, 0:128], ALU.subtract, name="iQtev")
    for jl in range(m):
        c.mm(ppi[:, 0:128], loc[hb:hb + 64, 128 * jl:128 * jl + 128], B("sQt")[hb:hb + 64, 0:128], name="iAc")
        c.ts(View(sloc, 0, 128, 512 * jl + 128 * m, 512 * jl + 128 * m + 128,
                  ap=sloc.h[:, 512 * jl + 128 * m:512 * jl + 128 * m + 128]),
             ppi[:, 0:128], SCALE, ALU.mult,
             bufs["GnS"][:, 2 * jl + h:2 * jl + h + 1], ALU.add, name="iAcm")
    c.mm(ppi[:, 0:128], WbT, KbT, name="iWK")
    c.tt(B("ssWK")[:, 0:128], ppi[:, 0:128], _c(bufs, "trilm1"), ALU.mult, name="iWKm")
    c.ts(B("ssKb")[:, 0:128], B("ssWK")[:, 0:128], bbc, ALU.mult, name="isKb")
    c.ts(B("ssKn")[:, 0:128], B("ssWK")[:, 0:128], nbbc, ALU.mult, name="isKn")
    c.mm(ppi[:, 0:128], KbT, QbT, start=True, stop=False, name="iAw1")
    c.mm(ppi[:, 0:128], B("ssKn")[:, 0:128], B("sY")[:, 0:128], start=False, stop=True, name="iAw2")
    c.ts(B("st")[:, 0:128], ppi[:, 0:128], SCALE, ALU.mult, gnc, ALU.add, name="iAwm")
    c.tt(View(sloc, 0, 128, 512 * m + 128 * m, 512 * m + 128 * m + 128,
              ap=sloc.h[:, 512 * m + 128 * m:512 * m + 128 * m + 128]),
         B("st")[:, 0:128], _c(bufs, "masknegS"), ALU.subtract, name="iAwmsk")
    c.mm(ppi[:, 0:128], B("sY")[:, 128:256], B("ssKb")[:, 0:128], name="iD")
    c.tt(B("sD")[:, 0:128], B("ssWK")[:, 0:128], ppi[:, 0:128], ALU.subtract, name="iDev")
    c.ts(B("sDb")[:, 0:128], B("sD")[:, 0:128], bbc, ALU.mult, name="iDb")
    c.mm(ppj[:, 0:64], B("sY")[:, 128:256], B("sWbb")[:, 0:64], name="iV")
    c.tt(B("sV")[:, 0:64], B("sWtok")[:, 0:64], ppj[:, 0:64], ALU.subtract, name="iVev")
    c.mm(ppj[hb:hb + 64, 64:128], B("sV")[:, 0:64], B("sWbb")[:, 0:64], name="iG")
    c.stt(B("sGTS")[hb:hb + 64, 0:64], ppj[hb:hb + 64, 64:128], -1.0,
          _c(bufs, "eye2", slice(hb, hb + 64), 64), ALU.mult, ALU.add, name="iGT")
    c.mm(pp3[0:64, 128:256], B("sCmS")[hb:hb + 64, 0:64], B("sQt")[hb:hb + 64, 0:128], name="iQtl")
    c.act(View(bufs[f"QnoM{h}"], 0, 64, 128 * m, 128 * m + 128,
               ap=bufs[f"QnoM{h}"].h[0:64, 128 * m:128 * m + 128]),
          pp3[0:64, 128:256], AF.Copy, scale=SCALE, name="iQtlev")
    if m > 0:
        c.mm(pp3[hb:hb + 64, 0:128 * m], B("sGTS")[hb:hb + 64, 0:64], loc[hb:hb + 64, 0:128 * m], name="iLKu")
        c.cp(loc[hb:hb + 64, 0:128 * m], pp3[hb:hb + 64, 0:128 * m], name="iLKuev")
    c.mm(pp3[hb:hb + 64, 384:512], B("sWtok")[:, 0:64], B("sDb")[:, 0:128], name="iKh")
    c.tt(loc[hb:hb + 64, 128 * m:128 * m + 128], KbT, pp3[hb:hb + 64, 384:512], ALU.subtract, name="iKhev")
    c.mm(ppj[hb:hb + 64, 0:64], B("sGTS")[hb:hb + 64, 0:64], B("sCmS")[hb:hb + 64, 0:64], name="iCm")
    c.mm(ppj[0:64, 128:192], B("sCmS")[hb:hb + 64, 0:64], B("sGTS")[hb:hb + 64, 0:64], name="iCmT")
    c.cp(B(f"CmT{h}")[0:64, 0:64], ppj[0:64, 128:192], name="iCmTev")
    c.cp(B("sCmS")[hb:hb + 64, 0:64], ppj[hb:hb + 64, 0:64], name="iCmev")


def emit_phaseA_group(c, bufs, Tloc, grp, fused=False):
    p = c.p
    noc = Tloc // CO
    KC = [bufs[f"KC{h}"] for h in range(2)]
    Vx = [bufs[f"Vx{h}"] for h in range(2)]
    Qe = [bufs[f"Qext{h}"] for h in range(2)]
    Grw = [bufs[f"Grw{h}"] for h in range(2)]
    Mrow = [bufs[f"Mrow{h}"] for h in range(2)]
    pozs = [bufs[f"poz{h}"] for h in range(2)]
    xbf, wprojS, wvS = bufs["xbfS"], bufs["wprojS"], bufs["wvS"]
    pp0, pp1, pp2, pp3 = bufs["pp0"], bufs["pp1"], bufs["pp2"], bufs["pp3"]
    ppi, ppj = bufs["ppi"], bufs["ppj"]
    cstS = bufs["cstS"]
    ident = _c(bufs, "ident")
    onesr = lambda w: cstS[0:1, CST["ones"][0]:CST["ones"][0] + w]
    onesc = cstS[slice(0, 128), CST["ones"][0]:CST["ones"][0] + 1]

    c.memset(bufs["localS"].full, 0.0)
    c.memset(bufs["wrw"][:, 0:3], 0.0)
    for h in range(2):
        vx3 = Vx[h].h.rearrange("p (b c) -> p b c", c=65)
        c.memset(View(Vx[h], 0, 128, 0, Vx[h].shape[1], ap=vx3[:, :, 64:65]), 1.0,
                 name="vxones")
    c.memset(bufs["carryS"].full, 0.0)
    for h in range(2):
        p.dma("SP", View(KC[h], 64, 67, 0, Tloc, ap=KC[h].h[64:67, 0:Tloc]),
              View(bufs["onesbf"], 0, 3, 0, Tloc, ap=bufs["onesbf"].h[0:3, 0:Tloc]),
              name="kcones")
        p.dma("SP", View(Qe[h], 67, 70, 0, 512, ap=Qe[h].h[67:70, 0:512]),
              View(bufs["onesbf"], 0, 3, 0, 512, ap=bufs["onesbf"].h[0:3, 0:512]),
              name="qeones")
        qn_ = bufs[f"QnoM{h}"]
        c.memset(View(qn_, 64, 67, 0, 512, ap=qn_.h[64:67, 0:512]), 0.0, name="qnz")
        p.dma("SP", View(qn_, 67, 70, 0, 512, ap=qn_.h[67:70, 0:512]),
              View(bufs["onesbf"], 0, 3, 0, 512, ap=bufs["onesbf"].h[0:3, 0:512]),
              name="qnones")

    for oc in range(noc):
        t0o = oc * CO
        x3 = bufs["xT"].h.rearrange("(dc p) t -> p dc t", p=128)
        xsrc = View(bufs["xT"], 0, 1 << 30, 0, 1 << 30, ap=x3[:, :, t0o:t0o + CO])
        xdst = View(xbf, 0, 128, 0, 16 * CO, ap=xbf.h.rearrange("p (dc t) -> p dc t", dc=16))
        p.dma("POOL", xdst, xsrc, name="ldx")
        for fam, (pp_, o0, o1) in enumerate(((pp0, 0, 128), (pp1, 128, 256), (pp2, 256, 288))):
            for dc in range(16):
                st, sp = dc == 0, dc == 15
                dst = pp_[0:32, 0:CO] if fam == 2 else pp_[:, 0:CO]
                c.mm(dst, wprojS[:, dc * 288 + o0:dc * 288 + o1],
                     xbf[:, dc * CO:(dc + 1) * CO], start=st, stop=sp, name="pq")
        for (pp, dstn, wci) in ((pp0, "qnT", 0), (pp1, "knT", 1)):
            dst = bufs[dstn]
            wcol = bufs["qknS"]
            raw = bufs["rawT"]
            c.cp(raw[:, 0:CO], pp[:, 0:CO], name="evqk")
            sq = bufs["sqT"]
            c.tt(sq[:, 0:CO], raw[:, 0:CO], raw[:, 0:CO], ALU.mult, name="sq")
            for h in range(2):
                hs = _c(bufs, "hselA" if h == 0 else "hselB")
                c.mm(pp3[0:1, 0:CO], hs, sq[:, 0:CO], name="ssq")
                c.act(bufs["r512a"][0:1, 0:CO], pp3[0:1, 0:CO], AF.Sqrt,
                      bias=_c(bufs, "c1e6", slice(0, 1)), scale=1.0 / HD, name="rstd")
                c.recip(bufs["r512b"][0:1, 0:CO], bufs["r512a"][0:1, 0:CO], name="rrec")
                c.mm(pp3[:, 0:CO], onesr(128), bufs["r512b"][0:1, 0:CO], name="rbc")
                c.stt(dst[64 * h:64 * h + 64, 0:CO], raw[64 * h:64 * h + 64, 0:CO],
                      wcol[64 * h:64 * h + 64, wci:wci + 1], pp3[64 * h:64 * h + 64, 0:CO],
                      ALU.mult, ALU.mult, name="rmsap")
        c.cp(bufs["xwsb"][0:32, 0:CO], pp2[0:32, 0:CO], name="evxw")
        c.mm(pp2[:, 0:CO], bufs["w2cS"][0:32, 0:128], bufs["xwsb"][0:32, 0:CO], name="wmm")
        wrw = bufs["wrw"]
        c.cp(wrw[:, 3:3 + CO], pp2[:, 0:CO], name="evw")
        wc_ = bufs["wcT"]
        cv = bufs["convS"]
        c.ts(wc_[:, 0:CO], wrw[:, 0:CO], cv[:, 0:1], ALU.mult, name="cv0")
        for k in range(1, 4):
            c.stt(wc_[:, 0:CO], wrw[:, k:k + CO], cv[:, k:k + 1], wc_[:, 0:CO],
                  ALU.mult, ALU.add, name=f"cv{k}")
        c.cp(wrw[:, 0:3], wrw[:, CO:CO + 3], name="wtail")
        sq = bufs["sqT"]
        c.tt(sq[:, 0:CO], wc_[:, 0:CO], wc_[:, 0:CO], ALU.mult, name="wsq")
        wnT = bufs["wnT"]
        for h in range(2):
            hs = _c(bufs, "hselA" if h == 0 else "hselB")
            c.mm(pp3[0:1, 0:CO], hs, sq[:, 0:CO], name="wssq")
            c.act(bufs["r512a"][0:1, 0:CO], pp3[0:1, 0:CO], AF.Sqrt,
                  bias=_c(bufs, "c1e12", slice(0, 1)), scale=1.0, name="wsrt")
            c.recip(bufs["r512b"][0:1, 0:CO], bufs["r512a"][0:1, 0:CO], name="wrec")
            c.mm(pp3[:, 0:CO], onesr(128), bufs["r512b"][0:1, 0:CO], name="wbc")
            c.tt(wnT[64 * h:64 * h + 64, 0:CO], wc_[64 * h:64 * h + 64, 0:CO],
                 pp3[64 * h:64 * h + 64, 0:CO], ALU.mult, name="wnap")
        for t4 in range(4):
            for dc in range(16):
                c.mm(pp2[:, 0:132], xbf[:, dc * CO + t4 * 128:dc * CO + t4 * 128 + 128],
                     wvS[:, dc * 132:dc * 132 + 132], start=(dc == 0), stop=False, name="pv")
            c.mm(pp2[:, 0:132], onesr(128), bufs["wvbS"][0:1, 0:132],
                 start=False, stop=True, name="pvb")
            kt = 4 * oc + t4
            for h in range(2):
                c.cp(View(Vx[h], 0, 128, kt * 65, kt * 65 + 64,
                          ap=Vx[h].h[:, kt * 65:kt * 65 + 64]),
                     pp2[:, 64 * h:64 * h + 64], name="vev")
            sg = bufs["sgS"]
            c.act(sg[:, 0:2], pp2[:, 128:130], AF.Sigmoid, name="btsg")
            c.ts(bufs["betaS"][:, 2 * t4:2 * t4 + 2], sg[:, 0:2], 2.0, ALU.mult, name="bt2")
            c.ts(bufs["nbetaS"][:, 2 * t4:2 * t4 + 2], sg[:, 0:2], -2.0, ALU.mult, name="btn")
            za = bufs["zaS"]
            c.act(za[:, 0:2], pp2[:, 130:132], AF.Abs, name="gabs")
            c.act(za[:, 2:4], za[:, 0:2], AF.Exp, scale=-1.0, name="gexp")
            c.act(za[:, 4:6], za[:, 2:4], AF.Ln, bias=1.0, name="gln")
            c.ts(za[:, 6:8], pp2[:, 130:132], 0.0, ALU.min, name="gmin")
            c.tt(bufs["gS"][:, 2 * t4:2 * t4 + 2], za[:, 6:8], za[:, 4:6],
                 ALU.subtract, name="gsub")
            c.mm(pp3[:, 0:2], _c(bufs, "cumIncl"), bufs["gS"][:, 2 * t4:2 * t4 + 2],
                 start=True, stop=False, name="gcum")
            c.mm(pp3[:, 0:2], onesr(128), bufs["carryS"][0:1, 0:2],
                 start=False, stop=True, name="gcar")
            c.cp(bufs["GtS"][:, 2 * t4:2 * t4 + 2], pp3[:, 0:2], name="gtev")
            c.ts(bufs["GnS"][:, 2 * t4:2 * t4 + 2], pp3[:, 0:2], -1.0, ALU.mult, name="gnev")
            c.mm(pp3[0:1, 0:2], onesc, bufs["gS"][:, 2 * t4:2 * t4 + 2],
                 start=True, stop=False, name="gcs")
            c.mm(pp3[0:1, 0:2], onesr(1), bufs["carryS"][0:1, 0:2],
                 start=False, stop=True, name="gcs2")
            c.cp(bufs["carryS"][0:1, 0:2], pp3[0:1, 0:2], name="gcarev")
            for h in range(2):
                c.tr(ppj[0:1, 0:128], bufs["GtS"][:, 2 * t4 + h:2 * t4 + h + 1],
                     ident, name="gtr")
                c.cp(Grw[h][0:1, t4 * 128:(t4 + 1) * 128], ppj[0:1, 0:128], name="grev")
        t1 = bufs["t2a"]
        b1f = bufs["t2b"]
        stg = bufs["stageG"]
        for h in range(2):
            hb_ = 64 * h
            hv = lambda f0, f1: View(stg, hb_, hb_ + 1, f0, f1, ap=stg.h[hb_:hb_ + 1, f0:f1])
            c.ts(t1[0:1, 0:CO], Grw[h][0:1, 0:CO], -1.0, ALU.mult, name="negG")
            c.cp(hv(0, 512), t1[0:1, 0:CO], name="spl1")
            c.cp(b1f[0:1, 0:CO], hv(0, 512), name="spl1f")
            c.tt(t1[0:1, 0:CO], t1[0:1, 0:CO], b1f[0:1, 0:CO], ALU.subtract, name="spl1r")
            c.cp(hv(512, 1024), t1[0:1, 0:CO], name="spl2")
            c.cp(b1f[0:1, 0:CO], hv(512, 1024), name="spl2f")
            c.tt(t1[0:1, 0:CO], t1[0:1, 0:CO], b1f[0:1, 0:CO], ALU.subtract, name="spl2r")
            c.cp(hv(1024, 1536), t1[0:1, 0:CO], name="spl3")
            p.dma("SP", View(KC[h], 67, 70, t0o, t0o + CO, ap=KC[h].h[67:70, t0o:t0o + CO]),
                  View(stg, hb_, hb_ + 1, 0, 1536, ap=stg.h[hb_:hb_ + 1, 0:1536]), name="gkc")
        for h in range(2):
            c.memset(bufs[f"sloc{h}"].full, -1e30)
            c.cp(bufs["sCmS"][64 * h:64 * h + 64, 0:64],
                 _c(bufs, "eye2", slice(64 * h, 64 * h + 64), 64), name="cminit")
        for m in range(MPO):
            for h in range(2):
                emit_inner(c, bufs, h, oc, m, Tloc)
        sq = bufs["sqT"]
        for h in range(2):
            mxq = bufs["mxq"]
            c.memset(mxq[:, 0:4], -1e30, name="mxinit")
            for qs in range(4):
                qv = View(bufs[f"QnoM{h}"], 0, 70, 128 * qs, 128 * qs + 128,
                          ap=bufs[f"QnoM{h}"].h[0:70, 128 * qs:128 * qs + 128])
                for kb in range(oc):
                    pS = pp0 if (kb % 2 == 0) else pp1
                    c.mm(pS[:, 0:CO], qv,
                         View(KC[h], 0, 70, kb * CO, (kb + 1) * CO,
                              ap=KC[h].h[0:70, kb * CO:(kb + 1) * CO]), name="xmm")
                    c.red(bufs["r128a"][:, 0:1], pS[:, 0:CO], ALU.max, name="xred")
                    c.tt(mxq[:, qs:qs + 1], mxq[:, qs:qs + 1], bufs["r128a"][:, 0:1],
                         ALU.max, name="xmax")
                for lt in range(qs + 1):
                    sv = View(bufs[f"sloc{h}"], 0, 128, 512 * lt + 128 * qs,
                              512 * lt + 128 * qs + 128,
                              ap=bufs[f"sloc{h}"].h[:, 512 * lt + 128 * qs:512 * lt + 128 * qs + 128])
                    c.tr(ppj[:, 0:128], sv, _c(bufs, "ident"), name="xtr")
                    c.red(bufs["r128a"][:, 0:1], ppj[:, 0:128], ALU.max, name="xredl")
                    c.tt(mxq[:, qs:qs + 1], mxq[:, qs:qs + 1], bufs["r128a"][:, 0:1],
                         ALU.max, name="xmaxl")
            for qs in range(4):
                c.tr(ppj[0:1, 0:128], mxq[:, qs:qs + 1], _c(bufs, "ident"), name="mtr")
                c.cp(Mrow[h][0:1, 128 * qs:128 * qs + 128], ppj[0:1, 0:128], name="mtev")
            c.cp(View(bufs[f"Qext{h}"], 0, 64, 0, CO, ap=bufs[f"Qext{h}"].h[0:64, 0:CO]),
                 View(bufs[f"QnoM{h}"], 0, 64, 0, CO, ap=bufs[f"QnoM{h}"].h[0:64, 0:CO]),
                 name="qecp")
            stm = bufs["stageM"]
            t1 = bufs["t2a"]
            b1f = bufs["t2b"]
            hb_ = 64 * h
            hv = lambda f0, f1: View(stm, hb_, hb_ + 1, f0, f1, ap=stm.h[hb_:hb_ + 1, f0:f1])
            c.ts(t1[0:1, 0:CO], Mrow[h][0:1, 0:CO], -1.0, ALU.mult, name="negM")
            c.cp(hv(0, 512), t1[0:1, 0:CO], name="mpl1")
            c.cp(b1f[0:1, 0:CO], hv(0, 512), name="mpl1f")
            c.tt(t1[0:1, 0:CO], t1[0:1, 0:CO], b1f[0:1, 0:CO], ALU.subtract, name="mpl1r")
            c.ts(bufs["r512a"][0:1, 0:CO], b1f[0:1, 0:CO], -1.0, ALU.mult, name="mt1")
            c.cp(hv(512, 1024), t1[0:1, 0:CO], name="mpl2")
            c.cp(b1f[0:1, 0:CO], hv(512, 1024), name="mpl2f")
            c.tt(t1[0:1, 0:CO], t1[0:1, 0:CO], b1f[0:1, 0:CO], ALU.subtract, name="mpl2r")
            c.tt(bufs["r512a"][0:1, 0:CO], bufs["r512a"][0:1, 0:CO],
                 b1f[0:1, 0:CO], ALU.subtract, name="mt2")
            c.cp(hv(1024, 1536), t1[0:1, 0:CO], name="mpl3")
            c.cp(b1f[0:1, 0:CO], hv(1024, 1536), name="mpl3f")
            c.tt(bufs["r512a"][0:1, 0:CO], bufs["r512a"][0:1, 0:CO],
                 b1f[0:1, 0:CO], ALU.subtract, name="mt3")
            p.dma("SP", View(bufs[f"Qext{h}"], 64, 67, 0, 512,
                             ap=bufs[f"Qext{h}"].h[64:67, 0:512]),
                  View(stm, hb_, hb_ + 1, 0, 1536, ap=stm.h[hb_:hb_ + 1, 0:1536]), name="mqe")
            c.mm(pp2[:, 0:CO], onesr(128), bufs["r512a"][0:1, 0:CO], name="mbc")
            c.cp(bufs[f"Mbc{h}"][:, 0:CO], pp2[:, 0:CO], name="mbcev")
            for lt in range(4):
                slv = View(bufs[f"sloc{h}"], 0, 128, 512 * lt, 512 * lt + 512,
                           ap=bufs[f"sloc{h}"].h[:, 512 * lt:512 * lt + 512])
                c.tt(sq[:, 0:CO], slv, bufs[f"Mbc{h}"][:, 0:CO], ALU.subtract, name="lsub")
                c.act(View(bufs[f"ploc{h}"], 0, 128, 512 * lt, 512 * lt + 512,
                           ap=bufs[f"ploc{h}"].h[:, 512 * lt:512 * lt + 512]),
                      sq[:, 0:CO], AF.Exp, name="lexp")
        for h in range(2):
            for kt in range(4 * oc):
                pS = pp0 if (kt % 2 == 0) else pp1
                c.mm(pS[:, 0:CO], View(KC[h], 0, 70, kt * 128, (kt + 1) * 128,
                                       ap=KC[h].h[0:70, kt * 128:(kt + 1) * 128]),
                     bufs[f"Qext{h}"][0:70, 0:CO], name="smm")
                pt = bufs["ptile"]
                c.act(pt[:, 0:CO], pS[:, 0:CO], AF.Exp, name="sexp")
                c.mm(pozs[h][0:65, 0:CO],
                     View(Vx[h], 0, 128, kt * 65, (kt + 1) * 65,
                          ap=Vx[h].h[:, kt * 65:(kt + 1) * 65]),
                     pt[:, 0:CO], start=(kt == 0), stop=False, name="ozmm")
            for lt in range(4):
                kt = 4 * oc + lt
                c.mm(pozs[h][0:65, 0:CO],
                     View(Vx[h], 0, 128, kt * 65, (kt + 1) * 65,
                          ap=Vx[h].h[:, kt * 65:(kt + 1) * 65]),
                     View(bufs[f"ploc{h}"], 0, 128, lt * CO, (lt + 1) * CO,
                          ap=bufs[f"ploc{h}"].h[:, lt * CO:(lt + 1) * CO]),
                     start=(oc == 0 and lt == 0), stop=(lt == 3), name="ozlmm")
            c.cp(bufs["ozsb"][0:65, 0:CO], pozs[h][0:65, 0:CO], name="ozev")
            p.dma("SP", bufs["zrow"][0:1, 0:CO], bufs["ozsb"][64:65, 0:CO], name="zmov")
            c.recip(bufs["zrec"][0:1, 0:CO], bufs["zrow"][0:1, 0:CO], name="zrc")
            if not fused:
                c.mm(pp2[0:64, 0:CO], onesr(64), bufs["zrec"][0:1, 0:CO], name="zbc")
                c.tt(bufs["onrm"][0:64, 0:CO], bufs["ozsb"][0:64, 0:CO], pp2[0:64, 0:CO],
                     ALU.mult, name="onrm")
                gh = grp * 2 + h
                p.dma("SP", View(bufs["oheads"], gh * 64, gh * 64 + 64, t0o, t0o + CO,
                                 ap=bufs["oheads"].h[gh * 64:(gh + 1) * 64, t0o:t0o + CO]),
                      bufs["onrm"][0:64, 0:CO], name="odma")
            else:
                c.mm(pp2[0:64, 0:CO], onesr(64), bufs["zrec"][0:1, 0:CO], name="zbc")
                c.tt(bufs[f"onrm{h}"][0:64, 0:CO], bufs["ozsb"][0:64, 0:CO],
                     pp2[0:64, 0:CO], ALU.mult, name="onrm2")
        if fused:
            for tt4 in range(4):
                for ec in range(4):
                    for h in range(2):
                        c.mm(bufs["ppw2"][:, 0:512],
                             bufs[f"onrm{h}"][0:64, 128 * tt4:128 * tt4 + 128],
                             bufs[f"woA{h}"][0:64, ec * 512:ec * 512 + 512],
                             start=(h == 0), stop=(h == 1), name="wopp")
                    c.cp(View(bufs["wrow"], 0, 128, ec * 512, ec * 512 + 512,
                              ap=bufs["wrow"].h[:, ec * 512:ec * 512 + 512]),
                         bufs["ppw2"][:, 0:512], name="woev")
                rv = View(bufs["rs1in"], 0, 1 << 30, 0, 1 << 30,
                          ap=bufs["rs1in"].h[t0o + 128 * tt4:t0o + 128 * tt4 + 128, 0:DM])
                if grp == 0:
                    p.dma("SP", rv, bufs["wrow"][slice(0, 128), slice(0, DM)], name="wodma")
                else:
                    p.dma("POOL", rv, bufs["wrow"][slice(0, 128), slice(0, DM)],
                          name="wodmaA", accum_op=ALU.add)

        for h in range(2):
            if oc > 0:
                c.cp(bufs[f"CmTb{h}"][0:64, 0:64], bufs[f"CmT{h}"][0:64, 0:64], name="cmtb")
            for j in range(oc):
                kv = View(KC[h], 0, 64, j * CO, (j + 1) * CO,
                          ap=KC[h].h[0:64, j * CO:(j + 1) * CO])
                c.mm(pp3[0:64, 0:CO], bufs[f"CmTb{h}"][0:64, 0:64], kv, name="kcu")
                c.cp(kv, pp3[0:64, 0:CO], name="kcuev")
            p.dma("POOL", View(KC[h], 0, 64, t0o, t0o + CO, ap=KC[h].h[0:64, t0o:t0o + CO]),
                  bufs["localS"][64 * h:64 * h + 64, 0:CO], name="kcins")
        c.memset(bufs["localS"].full, 0.0)


def build_phaseA(nc, Tloc, dbg=False):
    p = Prog(nc)
    c = Ctx(p)
    stack = []
    bufs = {}
    bufs["xT"] = dbuf(nc, "xT", (DM, Tloc), F32, kind="ExternalInput")
    bufs["oheads"] = dbuf(nc, "oheads", (256, Tloc), F32, kind="ExternalOutput")
    for g in range(2):
        bufs[f"wprojD{g}"] = dbuf(nc, f"wproj{g}", (DM, 288), F32, kind="ExternalInput")
        bufs[f"wvD{g}"] = dbuf(nc, f"wv{g}", (DM, 132), F32, kind="ExternalInput")
        bufs[f"w2cD{g}"] = dbuf(nc, f"w2c{g}", (32, 128), F32, kind="ExternalInput")
        bufs[f"convD{g}"] = dbuf(nc, f"conv{g}", (128, 4), F32, kind="ExternalInput")
        bufs[f"wvbD{g}"] = dbuf(nc, f"wvb{g}", (1, 132), F32, kind="ExternalInput")
    bufs["cstD"] = dbuf(nc, "cst", (128, CST_W), F32, kind="ExternalInput")
    bufs["qknD"] = dbuf(nc, "qkn", (128, 2), F32, kind="ExternalInput")
    bufs["onesbfD"] = dbuf(nc, "onesbfD", (3, Tloc), BF16, kind="ExternalInput")
    mk = lambda nm, shape, dt, sp='S': bufs.__setitem__(nm, mkbuf(p, nc, stack, "s_" + nm, shape, dt, sp))
    mk("cstS", (128, CST_W), F32)
    mk("qknS", (128, 2), F32)
    mk("onesbf", (3, Tloc), BF16)
    mk("wprojS", (128, 16 * 288), BF16)
    mk("wvS", (128, 16 * 132), BF16)
    mk("w2cS", (32, 128), F32)
    mk("convS", (128, 4), F32)
    mk("wvbS", (1, 132), F32)
    mk("xbfS", (128, 16 * CO), BF16)
    for h in range(2):
        mk(f"KC{h}", (70, Tloc), BF16)
        mk(f"Vx{h}", (128, (Tloc // 128) * 65), BF16)
        mk(f"Qext{h}", (70, 512), BF16)

        mk(f"ploc{h}", (128, 4 * CO), BF16)
        mk(f"Grw{h}", (1, CO), F32)
        mk(f"Mrow{h}", (1, CO), F32)
        mk(f"Mbc{h}", (128, CO), F32)
        mk(f"QnoM{h}", (70, 512), BF16)
        mk(f"sloc{h}", (128, 2048), F32)
        mk(f"CmT{h}", (64, 64), F32)
        mk(f"CmTb{h}", (64, 64), BF16)
    mk("localS", (128, 512), F32)
    mk("sCmS", (128, 64), F32)
    mk("sGTS", (128, 64), F32)
    mk("zrow", (1, CO), F32)
    mk("mxq", (128, 4), F32)
    mk("r128a", (128, 1), F32)
    mk("qnT", (128, CO), F32)
    mk("knT", (128, CO), F32)
    mk("wnT", (128, CO), F32)
    mk("rawT", (128, CO), F32)
    mk("sqT", (128, CO), F32)
    mk("sqK", (128, CO), F32)
    mk("wrw", (128, CO + 3), F32)
    mk("wcT", (128, CO), F32)
    mk("xwsb", (32, CO), F32)
    mk("betaS", (128, 8), F32)
    mk("nbetaS", (128, 8), F32)
    mk("gS", (128, 8), F32)
    mk("GnS", (128, 8), F32)
    mk("GtS", (128, 8), F32)
    mk("sgS", (128, 2), F32)
    mk("zaS", (128, 8), F32)
    mk("carryS", (1, 2), F32)
    mk("r512a", (1, CO), F32)
    mk("r512b", (1, CO), F32)
    mk("t2a", (1, CO), F32)
    mk("t2b", (1, CO), F32)
    mk("stageG", (65, 1536), BF16)
    mk("stageM", (65, 1536), BF16)
    mk("ptile", (128, CO), BF16)
    mk("ozsb", (65, CO), F32)
    mk("zrec", (1, CO), F32)
    mk("onrm", (64, CO), F32)
    for nm in ("sWW", "sbd", "sO", "sLbd", "sdb", "sLbdT", "sL2", "sL2T",
               "sL4", "sL4T", "sL8", "sP", "sPT", "sN", "sRb", "ssWK",
               "ssKb", "ssKn", "sD", "sDb", "st"):
        mk(nm, (128, 128), F32)
    for nm in ("sX", "sB", "sY"):
        mk(nm, (128, 256), F32)
    mk("sWtok", (128, 64), F32)
    mk("sWbb", (128, 64), F32)
    mk("sV", (128, 64), F32)
    mk("sQt", (128, 128), F32)
    for nm in ("pp0", "pp1", "pp2", "pp3", "ppi", "ppj"):
        mk(nm, (128, 512), F32, 'P')
    mk("poz0", (65, 512), F32, 'P')
    mk("poz1", (65, 512), F32, 'P')

    p.dma("SP", bufs["cstS"].full, bufs["cstD"].full, name="lcst")
    p.dma("SP", bufs["qknS"].full, bufs["qknD"].full, name="lqkn")
    p.dma("POOL", bufs["onesbf"].full, bufs["onesbfD"].full, name="lonesbf")

    for g in range(2):
        wsrc = View(bufs[f"wprojD{g}"], 0, 1 << 30, 0, 1 << 30,
                    ap=bufs[f"wprojD{g}"].h.rearrange("(dc p) m -> p dc m", p=128))
        wdst = View(bufs["wprojS"], 0, 128, 0, 16 * 288,
                    ap=bufs["wprojS"].h.rearrange("p (dc m) -> p dc m", dc=16))
        p.dma("POOL", wdst, wsrc, name="lwproj")
        vsrc = View(bufs[f"wvD{g}"], 0, 1 << 30, 0, 1 << 30,
                    ap=bufs[f"wvD{g}"].h.rearrange("(dc p) m -> p dc m", p=128))
        vdst = View(bufs["wvS"], 0, 128, 0, 16 * 132,
                    ap=bufs["wvS"].h.rearrange("p (dc m) -> p dc m", dc=16))
        p.dma("POOL", vdst, vsrc, name="lwv")
        p.dma("SP", bufs["w2cS"].full, bufs[f"w2cD{g}"].full, name="lw2c")
        p.dma("SP", bufs["convS"].full, bufs[f"convD{g}"].full, name="lconv")
        p.dma("SP", bufs["wvbS"].full, bufs[f"wvbD{g}"].full, name="lwvb")
        emit_phaseA_group(c, bufs, Tloc, g)
        if g == 0 and dbg:
            for nm, shp in (("qnT", (128, CO)), ("knT", (128, CO)), ("wnT", (128, CO)),
                            ("betaS", (128, 8)), ("gS", (128, 8)), ("GtS", (128, 8)),
                            ("Grw0", (1, CO)), ("Mrow0", (1, CO)), ("Mrow1", (1, CO)),
                            ("ozsb", (65, CO)), ("zrec", (1, CO)), ("onrm", (64, CO)),
                            ("sloc0", (128, 2048)), ("sloc1", (128, 2048)),
                            ("localS", (128, 512))):
                db = dbuf(nc, "dbg_" + nm, shp, F32, kind="ExternalOutput")
                p.dma("SP", db.full, bufs[nm][0:shp[0], 0:shp[1]], name="dbg")
            for nm, shp in (("Qext0", (70, 512)), ("KC0", (70, Tloc)), ("ploc0", (128, 2048)),
                            ("ploc1", (128, 2048)),
                            ("Vx0", (128, (Tloc // 128) * 65)), ("Vx1", (128, (Tloc // 128) * 65))):
                db = dbuf(nc, "dbg_" + nm, shp, BF16, kind="ExternalOutput")
                p.dma("SP", db.full, View(bufs[nm], 0, shp[0], 0, shp[1],
                                          ap=bufs[nm].h[0:shp[0], 0:shp[1]]), name="dbg")
        if g == 0:
            p.barrier("grp")
    p.barrier("fin")
    nops = p.emit()
    for cm in reversed(stack):
        cm.__exit__(None, None, None)
    return nops


def prep_A_inputs(inputs, core, Tloc):
    f = np.float32
    x = np.asarray(inputs["L0_attn_hidden_states"], f).reshape(T, DM)[:Tloc]
    xT = np.ascontiguousarray(x.T)
    Wq = np.asarray(inputs["L0_attn_Wq"], f)
    Wk = np.asarray(inputs["L0_attn_Wk"], f)
    Wv = np.asarray(inputs["L0_attn_Wv"], f)
    W1 = np.asarray(inputs["L0_attn_W1"], f)
    W2 = np.asarray(inputs["L0_attn_W2"], f)
    btw = np.asarray(inputs["L0_attn_bt_w"], f)
    btb = np.asarray(inputs["L0_attn_bt_b"], f)
    gw = np.asarray(inputs["L0_attn_g_w"], f)
    gb = np.asarray(inputs["L0_attn_g_b"], f)
    cw = np.asarray(inputs["L0_attn_conv_w"], f)
    d = {"xT": xT, "cst": host_consts(),
         "onesbfD": np.ones((3, Tloc), ml_bf16())}
    qkn = np.zeros((128, 2), f)
    qkn[:, 0] = np.tile(np.asarray(inputs["L0_attn_q_norm_w"], f), 2)
    qkn[:, 1] = np.tile(np.asarray(inputs["L0_attn_k_norm_w"], f), 2)
    d["qkn"] = qkn
    for g in range(2):
        hA = 4 * core + 2 * g
        hB = hA + 1
        wp = np.zeros((DM, 288), f)
        wp[:, 0:64] = Wq[64 * hA:64 * hA + 64].T
        wp[:, 64:128] = Wq[64 * hB:64 * hB + 64].T
        wp[:, 128:192] = Wk[64 * hA:64 * hA + 64].T
        wp[:, 192:256] = Wk[64 * hB:64 * hB + 64].T
        wp[:, 256:288] = W1.T
        d[f"wproj{g}"] = wp
        wv = np.zeros((DM, 132), f)
        wv[:, 0:64] = Wv[64 * hA:64 * hA + 64].T
        wv[:, 64:128] = Wv[64 * hB:64 * hB + 64].T
        wv[:, 128] = btw[hA]
        wv[:, 129] = btw[hB]
        wv[:, 130] = gw[hA]
        wv[:, 131] = gw[hB]
        d[f"wv{g}"] = wv
        wvb = np.zeros((1, 132), f)
        wvb[0, 128] = btb[hA]; wvb[0, 129] = btb[hB]
        wvb[0, 130] = gb[hA]; wvb[0, 131] = gb[hB]
        d[f"wvb{g}"] = wvb
        w2c = np.zeros((32, 128), f)
        w2c[:, 0:64] = W2[64 * hA:64 * hA + 64].T
        w2c[:, 64:128] = W2[64 * hB:64 * hB + 64].T
        d[f"w2c{g}"] = w2c
        cg = np.zeros((128, 4), f)
        cg[0:64] = cw[64 * hA:64 * hA + 64]
        cg[64:128] = cw[64 * hB:64 * hB + 64]
        d[f"conv{g}"] = cg
    return d


def ml_bf16():
    import ml_dtypes
    return ml_dtypes.bfloat16



def _ln_emit(c, bufs, src_chunks, dst_chunks, dstb_chunks, nch, NT, eps, name,
             sumsq_out=None):
    p = c.p
    ppw = bufs["ppw"]
    onesr = lambda w: bufs["cstS"][0:1, CST["ones"][0]:CST["ones"][0] + w]
    onescol = bufs["cstS"][slice(0, 128), CST["ones"][0]:CST["ones"][0] + 1]
    sq = bufs["lnsq"]
    def halves(v):
        b_ = v.buf
        return [View(b_, v.p0, v.p1, v.f0 + th * 512, v.f0 + th * 512 + 512,
                     ap=b_.h[v.p0:v.p1, v.f0 + th * 512:v.f0 + th * 512 + 512]) for th in range(2)]
    for ic in range(nch):
        for th, hv_ in enumerate(halves(src_chunks[ic])):
            c.mm(ppw[0:1, th * 512:th * 512 + 512], onescol, hv_,
                 start=(ic == 0), stop=(ic == nch - 1), name=name + "sum")
    c.cp(bufs["lnsum"][0:1, 0:NT], ppw[0:1, 0:NT], name=name + "sumev")
    for ic in range(nch):
        c.act(sq[:, 0:NT], src_chunks[ic], AF.Square, name=name + "sq")
        for th in range(2):
            c.mm(ppw[0:1, th * 512:th * 512 + 512], onescol, sq[:, th * 512:th * 512 + 512],
                 start=(ic == 0), stop=(ic == nch - 1), name=name + "ssq")
    c.cp(bufs["lnssq"][0:1, 0:NT], ppw[0:1, 0:NT], name=name + "ssqev")
    if sumsq_out is not None:
        c.cp(sumsq_out, bufs["lnssq"][0:1, 0:NT], name=name + "ssqcp")
    D = nch * 128
    c.ts(bufs["lnmu"][0:1, 0:NT], bufs["lnsum"][0:1, 0:NT], 1.0 / D, ALU.mult, name=name + "mu")
    c.tt(bufs["lnt"][0:1, 0:NT], bufs["lnmu"][0:1, 0:NT], bufs["lnmu"][0:1, 0:NT], ALU.mult, name=name + "mu2")
    c.stt(bufs["lnvar"][0:1, 0:NT], bufs["lnssq"][0:1, 0:NT], 1.0 / D, bufs["lnt"][0:1, 0:NT],
          ALU.mult, ALU.subtract, name=name + "var")
    c.act(bufs["lnt"][0:1, 0:NT], bufs["lnvar"][0:1, 0:NT], AF.Sqrt,
          bias=_c(bufs, "c1e5", slice(0, 1)), name=name + "sd")
    c.recip(bufs["lnrs"][0:1, 0:NT], bufs["lnt"][0:1, 0:NT], name=name + "rs")
    for th in range(2):
        c.mm(ppw[:, th * 512:th * 512 + 512], onesr(128), bufs["lnmu"][0:1, th * 512:th * 512 + 512], name=name + "bmu")
    c.cp(bufs["lnmub"][:, 0:NT], ppw[:, 0:NT], name=name + "bmuev")
    for th in range(2):
        c.mm(ppw[:, th * 512:th * 512 + 512], onesr(128), bufs["lnrs"][0:1, th * 512:th * 512 + 512], name=name + "brs")
    c.cp(bufs["lnrsb"][:, 0:NT], ppw[:, 0:NT], name=name + "brsev")
    for ic in range(nch):
        c.tt(bufs["lnsq"][:, 0:NT], src_chunks[ic], bufs["lnmub"][:, 0:NT], ALU.subtract, name=name + "c1")
        c.tt(dst_chunks[ic], bufs["lnsq"][:, 0:NT], bufs["lnrsb"][:, 0:NT], ALU.mult, name=name + "c2")
        if dstb_chunks is not None:
            c.cp(dstb_chunks[ic], dst_chunks[ic], name=name + "cb")


def build_phaseB(nc):
    NT = TPC
    p = Prog(nc)
    c = Ctx(p)
    stack = []
    bufs = {}
    mk = lambda nm, shape, dt, sp='S': bufs.__setitem__(nm, mkbuf(p, nc, stack, "s_" + nm, shape, dt, sp))
    dd = lambda nm, shape, dt, kind="ExternalInput": dbuf(nc, nm, shape, dt, kind=kind)
    bufs["oT"] = dd("oT", (DM, NT), F32)
    bufs["x0T"] = dd("x0T", (DM, NT), F32)
    bufs["WoT"] = dd("WoT", (DM, DM), F32)
    bufs["swA"] = dd("swA", (DM, 64), F32)
    bufs["mdT"] = dd("mdT", (DM, 8), F32)
    bufs["updA"] = dd("updA", (DM, 64), F32)
    bufs["dgnA"] = dd("dgnA", (64, DM), F32)
    bufs["biasW"] = dd("biasW", (8, DM), F32)
    bufs["dbT"] = dd("dbT", (DM, 8), F32)
    bufs["impc"] = dd("impc", (8, 2), F32)
    bufs["cstD"] = dd("cstB", (128, CST_W), F32)
    bufs["x2T"] = dd("x2T", (DM, NT), F32, kind="ExternalOutput")
    stackA = []
    mkA = lambda nm, shape, dt, sp='S': bufs.__setitem__(nm, mkbuf(p, nc, stackA, "s_" + nm, shape, dt, sp))
    mk("cstS", (128, CST_W), F32)
    mk("xT", (128, 16 * NT), F32)
    mkA("WoTS", (128, 2 * 16 * 128), BF16)
    mkA("obfS", (128, 16 * NT), BF16)
    for nm in ("pp0", "ppq", "pp3"):
        mk(nm, (128, 512), F32, 'P')
    mk("ppw", (128, 1024), F32, 'P')
    mk("pps", (64, 1024), F32, 'P')

    onesr = lambda w: bufs["cstS"][0:1, CST["ones"][0]:CST["ones"][0] + w]
    p.dma("SP", bufs["cstS"].full, bufs["cstD"].full, name="lcst")
    for (dst, src, w) in (("obfS", "oT", NT),):
        sv = View(bufs[src], 0, 1 << 30, 0, 1 << 30,
                  ap=bufs[src].h.rearrange("(dc p) m -> p dc m", p=128))
        dv = View(bufs[dst], 0, 128, 0, 16 * w,
                  ap=bufs[dst].h.rearrange("p (dc m) -> p dc m", dc=16))
        p.dma("POOL", dv, sv, name="l" + dst)
    xsv = View(bufs["x0T"], 0, 1 << 30, 0, 1 << 30,
               ap=bufs["x0T"].h.rearrange("(dc p) m -> p dc m", p=128))
    p.dma("SP", View(bufs["xT"], 0, 128, 0, 16 * NT,
                     ap=bufs["xT"].h.rearrange("p (dc m) -> p dc m", dc=16)), xsv, name="lx")

    xch = lambda buf, ic, w=NT: bufs[buf][slice(0, 128), slice(ic * w, ic * w + w)]
    wsrc3 = bufs["WoT"].h.rearrange("(dc p) m -> p dc m", p=128)
    wdst3 = bufs["WoTS"].h.rearrange("p (r dc m) -> p r dc m", r=2, dc=16)
    for ec in range(16):
        rr = ec % 2
        p.dma("POOL", View(bufs["WoTS"], 0, 128, rr * 2048, rr * 2048 + 2048,
                           ap=wdst3[:, rr]),
              View(bufs["WoT"], 0, 1 << 30, 0, 1 << 30,
                   ap=wsrc3[:, :, ec * 128:(ec + 1) * 128]), name="lwo")
        for th in range(2):
            pp = bufs["ppw"]
            for dc2 in range(16):
                c.mm(pp[:, 0:512], bufs["WoTS"][:, rr * 2048 + dc2 * 128:rr * 2048 + dc2 * 128 + 128],
                     bufs["obfS"][:, dc2 * NT + th * 512:dc2 * NT + th * 512 + 512],
                     start=(dc2 == 0), stop=(dc2 == 15), name="womm")
            c.tt(View(bufs["xT"], 0, 128, ec * NT + th * 512, ec * NT + th * 512 + 512,
                      ap=bufs["xT"].h[:, ec * NT + th * 512:ec * NT + th * 512 + 512]),
                 xch("xT", ec)[slice(0, 128), slice(th * 512, th * 512 + 512)] if False else
                 View(bufs["xT"], 0, 128, ec * NT + th * 512, ec * NT + th * 512 + 512,
                      ap=bufs["xT"].h[:, ec * NT + th * 512:ec * NT + th * 512 + 512]),
                 pp[:, 0:512], ALU.add, name="wores")
    p.barrier("woend")
    for cm in reversed(stackA):
        cm.__exit__(None, None, None)
    mk("x1n", (128, 16 * NT), F32)
    mk("swS", (128, 16 * 64), F32)
    mk("updS", (128, 16 * 64), F32)
    mk("dgnS", (64, 16 * 128), BF16)
    mk("biasS", (8, 16 * 128), BF16)
    mk("mdS", (128, 16 * 8), F32)
    mk("dbS", (128, 16 * 8), F32)
    mk("impS", (8, 2), F32)
    mk("lnsq", (128, NT), F32)
    for nm in ("lnsum", "lnssq", "lnmu", "lnvar", "lnt", "lnrs", "sumsqx"):
        mk(nm, (1, NT), F32)
    mk("lnmub", (128, NT), F32)
    mk("lnrsb", (128, NT), F32)
    mk("cswS", (64, 8), F32)
    mk("cswC", (64, 1), F32)
    mk("udbC", (64, 1), F32)
    mk("md2S", (8, 8), F32)
    mk("md2C", (8, 1), F32)
    mk("r2S", (8, NT), F32)
    mk("gTt", (128, 8), F32)
    mk("selc", (128, 8), F32)
    mk("indt", (128, 8), F32)
    mk("indS", (8, NT), BF16)
    mk("mxiS", (64, NT), BF16)
    for (dst, src, w) in (("swS", "swA", 64), ("updS", "updA", 64),
                          ("mdS", "mdT", 8), ("dbS", "dbT", 8)):
        sv = View(bufs[src], 0, 1 << 30, 0, 1 << 30,
                  ap=bufs[src].h.rearrange("(dc p) m -> p dc m", p=128))
        dv = View(bufs[dst], 0, 128, 0, 16 * w,
                  ap=bufs[dst].h.rearrange("p (dc m) -> p dc m", dc=16))
        p.dma("POOL", dv, sv, name="l" + dst)
    sv = View(bufs["dgnA"], 0, 1 << 30, 0, 1 << 30,
              ap=bufs["dgnA"].h.rearrange("n (dc m) -> n dc m", dc=16))
    p.dma("POOL", View(bufs["dgnS"], 0, 64, 0, 16 * 128,
                       ap=bufs["dgnS"].h.rearrange("n (dc m) -> n dc m", dc=16)), sv, name="ldgn")
    sv = View(bufs["biasW"], 0, 1 << 30, 0, 1 << 30,
              ap=bufs["biasW"].h.rearrange("n (dc m) -> n dc m", dc=16))
    p.dma("POOL", View(bufs["biasS"], 0, 8, 0, 16 * 128,
                       ap=bufs["biasS"].h.rearrange("n (dc m) -> n dc m", dc=16)), sv, name="lbw")
    p.dma("SP", bufs["impS"].full, bufs["impc"].full, name="limp")
    _ln_emit(c, bufs, [xch("xT", ic) for ic in range(16)],
             [xch("x1n", ic) for ic in range(16)], None, 16, NT, 1e-5, "ln1")
    emit_B_ffn(c, bufs, NT)
    _ln_emit(c, bufs, [xch("xT", ic) for ic in range(16)],
             [xch("x1n", ic) for ic in range(16)], None, 16, NT, 1e-5, "ln2")
    for ic in range(16):
        dv = View(bufs["x2T"], 0, 1 << 30, 0, 1 << 30, ap=bufs["x2T"].h[ic * 128:(ic + 1) * 128, 0:NT])
        p.dma("SP", dv, xch("x1n", ic), name="sx2")
    p.barrier("fin")
    nops = p.emit()
    for cm in reversed(stack):
        cm.__exit__(None, None, None)
    return nops


def prep_B_inputs(inputs, core, oT_full):
    f = np.float32
    d = {}
    t0 = core * TPC
    d["oT"] = np.ascontiguousarray(oT_full[:, t0:t0 + TPC])
    x = np.asarray(inputs["L0_attn_hidden_states"], f).reshape(T, DM)
    d["x0T"] = np.ascontiguousarray(x[t0:t0 + TPC].T)
    d["WoT"] = np.ascontiguousarray(np.asarray(inputs["L0_attn_Wo"], f).T)
    sw = np.asarray(inputs["L0_ffn_select_weight"], f)
    d["swA"] = np.ascontiguousarray(sw.transpose(1, 0, 2).reshape(DM, 64))
    d["mdT"] = np.ascontiguousarray(np.asarray(inputs["L0_ffn_select_mean_diff"], f).T)
    upd = np.asarray(inputs["L0_ffn_lora_update"], f)
    d["updA"] = np.ascontiguousarray(upd.transpose(1, 0, 2).reshape(DM, 64))
    dgn = np.asarray(inputs["L0_ffn_lora_degen"], f)
    d["dgnA"] = np.ascontiguousarray(dgn.transpose(0, 2, 1).reshape(64, DM))
    d["biasW"] = np.asarray(inputs["L0_ffn_bias_w"], f)
    d["dbT"] = np.ascontiguousarray(np.asarray(inputs["L0_ffn_debias_w"], f).T)
    impc = np.zeros((8, 2), f)
    impc[:, 0] = np.asarray(inputs["L0_ffn_imp_center"], f)
    impc[:, 1] = np.asarray(inputs["L0_ffn_imp_slope"], f)
    d["impc"] = impc
    d["cstB"] = host_consts()
    return d



def build_phaseC(nc):
    p = Prog(nc)
    c = Ctx(p)
    stack = []
    bufs = {}
    mk = lambda nm, shape, dt, sp='S': bufs.__setitem__(nm, mkbuf(p, nc, stack, "s_" + nm, shape, dt, sp))
    NB = T // 128
    bufs["x2F"] = dbuf(nc, "x2F", (DM, T), F32, kind="ExternalInput")
    bufs["Wzc"] = dbuf(nc, "Wzc", (DM, 512), F32, kind="ExternalInput")
    bufs["Wxba"] = dbuf(nc, "Wxba", (DM, 516), F32, kind="ExternalInput")
    bufs["Dh"] = dbuf(nc, "Dh", (128, 2), F32, kind="ExternalInput")
    bufs["cstD"] = dbuf(nc, "cstC", (128, CST_W), F32, kind="ExternalInput")
    bufs["identbD"] = dbuf(nc, "identb", (128, 128), BF16, kind="ExternalInput")
    bufs["yzO"] = dbuf(nc, "yzO", (128, 2 * T), BF16, kind="ExternalOutput")
    mk("cstS", (128, CST_W), F32)
    mk("identb", (128, 128), BF16)
    mk("DhS", (128, 2), F32)
    mk("WzcS", (128, 16 * 512), BF16)
    mk("WxbaS", (128, 16 * 516), BF16)
    mk("xbfS", (128, 16 * 512), BF16)
    mk("zT", (128, 2 * 512), BF16)
    mk("cT", (128, 2 * 512), BF16)
    mk("xtok", (128, 4 * 256), BF16)
    mk("btok", (128, 4 * 256), BF16)
    mk("atok", (128, 4 * 4), F32)
    mk("yzT", (128, 2 * 128), BF16)
    mk("SSf", (128, 2 * 64), F32)
    mk("SSb", (128, 2 * 64), BF16)
    mk("ldc", (128, 1), F32)
    mk("wcol", (128, 1), F32)
    mk("wrow", (1, 128), F32)
    mk("nwrow", (1, 128), F32)
    mk("wlb", (128, 1), F32)
    mk("ewl", (1, 1), F32)
    mk("ewlb", (64, 1), F32)
    mk("ewsc", (128, 1), F32)
    mk("ewrow", (1, 128), F32)
    mk("Dxp", (128, 128), F32)
    mk("Rsb", (128, 128), BF16)
    mk("bTd", (128, 128), BF16)
    mk("csb", (128, 128), BF16)
    mk("bsc", (128, 64), BF16)
    mk("silz", (128, 128), F32)
    mk("xTd", (128, 128), F32)
    mk("t64", (128, 128), F32)
    mk("ewlb2", (128, 1), F32)
    mk("tot", (1, 1), F32)
    for nm in ("pc0", "pc1", "pc2"):
        mk(nm, (128, 512), F32, 'P')
    mk("pcy", (128, 512), F32, "P")
    mk("pcb", (128, 256), BF16, 'P')

    onesr = lambda w: bufs["cstS"][0:1, CST["ones"][0]:CST["ones"][0] + w]
    onescol = bufs["cstS"][slice(0, 128), CST["ones"][0]:CST["ones"][0] + 1]
    ident = _c(bufs, "ident")
    p.dma("SP", bufs["cstS"].full, bufs["cstD"].full, name="lcst")
    p.dma("SP", bufs["identb"].full, bufs["identbD"].full, name="lidb")
    p.dma("SP", bufs["DhS"].full, bufs["Dh"].full, name="lDh")
    for (dst, src, w) in (("WzcS", "Wzc", 512), ("WxbaS", "Wxba", 516)):
        sv = View(bufs[src], 0, 1 << 30, 0, 1 << 30,
                  ap=bufs[src].h.rearrange("(dc p) m -> p dc m", p=128))
        dv = View(bufs[dst], 0, 128, 0, 16 * w,
                  ap=bufs[dst].h.rearrange("p (dc m) -> p dc m", dc=16))
        p.dma("POOL", dv, sv, name="l" + dst)
    x3 = bufs["x2F"].h.rearrange("(dc p) t -> p dc t", p=128)
    c.memset(bufs["SSf"].full, 0.0)
    c.memset(bufs["SSb"].full, 0.0)
    for blk in range(T // 512):
        t0 = blk * 512
        xsrc = View(bufs["x2F"], 0, 1 << 30, 0, 1 << 30, ap=x3[:, :, t0:t0 + 512])
        xdst = View(bufs["xbfS"], 0, 128, 0, 16 * 512,
                    ap=bufs["xbfS"].h.rearrange("p (dc t) -> p dc t", dc=16))
        p.dma("POOL", xdst, xsrc, name="ldx")
        for mt in range(4):
            for dc in range(16):
                c.mm(bufs["pc0"][:, 0:512], bufs["WzcS"][:, dc * 512 + mt * 128:dc * 512 + mt * 128 + 128],
                     bufs["xbfS"][:, dc * 512:(dc + 1) * 512],
                     start=(dc == 0), stop=(dc == 15), name="pzc")
            dstb = bufs["zT"] if mt < 2 else bufs["cT"]
            hp = mt % 2
            c.cp(View(dstb, 0, 128, hp * 512, hp * 512 + 512,
                      ap=dstb.h[:, hp * 512:hp * 512 + 512]),
                 bufs["pc0"][:, 0:512], name="zcev")
        for t4 in range(4):
            for half in range(2):
                for dc in range(16):
                    c.mm(bufs["pc1"][:, half * 256:half * 256 + 256],
                         bufs["xbfS"][:, dc * 512 + t4 * 128:dc * 512 + t4 * 128 + 128],
                         bufs["WxbaS"][:, dc * 516 + half * 256:dc * 516 + half * 256 + 256],
                         start=(dc == 0), stop=(dc == 15), name="pxb")
            for dc in range(16):
                c.mm(bufs["pc2"][:, 0:4],
                     bufs["xbfS"][:, dc * 512 + t4 * 128:dc * 512 + t4 * 128 + 128],
                     bufs["WxbaS"][:, dc * 516 + 512:dc * 516 + 516],
                     start=(dc == 0), stop=(dc == 15), name="pa")
            c.cp(View(bufs["xtok"], 0, 128, t4 * 256, t4 * 256 + 256,
                      ap=bufs["xtok"].h[:, t4 * 256:t4 * 256 + 256]), bufs["pc1"][:, 0:256], name="xtev")
            c.cp(View(bufs["btok"], 0, 128, t4 * 256, t4 * 256 + 256,
                      ap=bufs["btok"].h[:, t4 * 256:t4 * 256 + 256]), bufs["pc1"][:, 256:512], name="btev")
            c.cp(View(bufs["atok"], 0, 128, t4 * 4, t4 * 4 + 4,
                      ap=bufs["atok"].h[:, t4 * 4:t4 * 4 + 4]), bufs["pc2"][:, 0:4], name="atev")
        for t4 in range(4):
            ch = 4 * blk + t4
            for h in range(HPC):
                hp, hl = h // 2, h % 2
                hb = 64 * hl
                acol = bufs["atok"][slice(0, 128), slice(t4 * 4 + h, t4 * 4 + h + 1)]
                c.act(bufs["ldc"][:, 0:1], acol, AF.Exp, name="cEa")
                c.ts(bufs["ldc"][:, 0:1], bufs["ldc"][:, 0:1], -1.0, ALU.mult, name="cNeg")
                c.mm(bufs["pc2"][:, 0:1], _c(bufs, "cumIncl"), bufs["ldc"][:, 0:1], name="cWc")
                c.cp(bufs["wcol"][:, 0:1], bufs["pc2"][:, 0:1], name="cWcev")
                c.mm(bufs["pc2"][0:1, 4:132], bufs["ldc"][:, 0:1], _c(bufs, "cumIncl"), name="cWr")
                c.cp(bufs["wrow"][0:1, 0:128], bufs["pc2"][0:1, 4:132], name="cWrev")
                c.ts(bufs["nwrow"][0:1, 0:128], bufs["wrow"][0:1, 0:128], -1.0, ALU.mult, name="cNwr")
                c.mm(bufs["pc2"][0:1, 132:133], onescol, bufs["ldc"][:, 0:1], name="cTot")
                c.cp(bufs["tot"][0:1, 0:1], bufs["pc2"][0:1, 132:133], name="cTotev")
                c.act(bufs["ewl"][0:1, 0:1], bufs["tot"][0:1, 0:1], AF.Exp, name="cEwl")
                c.mm(bufs["pc2"][:, 136:137], onesr(128), bufs["ewl"][0:1, 0:1], name="cEwlb")
                c.cp(bufs["ewlb2"][:, 0:1], bufs["pc2"][:, 136:137], name="cEwlbev")
                c.mm(bufs["pc2"][:, 140:141], onesr(128), bufs["tot"][0:1, 0:1], name="cWlb")
                c.cp(bufs["wlb"][:, 0:1], bufs["pc2"][:, 140:141], name="cWlbev")
                p.op("ACT", lambda e, b=bufs: e.activation(b["ewsc"].h[:, 0:1], b["wcol"].h[:, 0:1],
                                                           AF.Exp, bias=b["wlb"].h[:, 0:1], scale=-1.0),
                     reads=[bufs["wcol"][:, 0:1], bufs["wlb"][:, 0:1]],
                     writes=[bufs["ewsc"][:, 0:1]], name="cEws")
                c.act(bufs["ewrow"][0:1, 0:128], bufs["wrow"][0:1, 0:128], AF.Exp, name="cEwr")
                c.mm(bufs["pc0"][:, 0:128], bufs["nwrow"][0:1, 0:128], onesr(128),
                     start=True, stop=False, name="cD1")
                c.mm(bufs["pc0"][:, 0:128], onesr(128), bufs["wrow"][0:1, 0:128],
                     start=False, stop=True, name="cD2")
                c.tt(bufs["Dxp"][:, 0:128], bufs["pc0"][:, 0:128], _c(bufs, "masknegS"), ALU.subtract, name="cDm")
                c.act(bufs["Dxp"][:, 0:128], bufs["Dxp"][:, 0:128], AF.Exp, name="cDe")
                btv = View(bufs["btok"], 0, 128, t4 * 256 + 64 * h, t4 * 256 + 64 * h + 64,
                           ap=bufs["btok"].h[:, t4 * 256 + 64 * h:t4 * 256 + 64 * h + 64])
                c.tr(bufs["pcb"][hb:hb + 64, 0:128], btv, bufs["identb"][slice(0, 128), slice(0, 128)], name="cBtr")
                c.cp(bufs["bTd"][hb:hb + 64, 0:128], bufs["pcb"][hb:hb + 64, 0:128], name="cBtrev")
                ctv = View(bufs["cT"], hb, hb + 64, hp * 512 + t4 * 128, hp * 512 + t4 * 128 + 128,
                           ap=bufs["cT"].h[hb:hb + 64, hp * 512 + t4 * 128:hp * 512 + t4 * 128 + 128])
                c.mm(bufs["pc1"][:, 128:256], bufs["bTd"][hb:hb + 64, 0:128], ctv, name="cCB")
                c.tt(bufs["Rsb"][:, 0:128], bufs["Dxp"][:, 0:128], bufs["pc1"][:, 128:256], ALU.mult, name="cR")
                c.mm(bufs["pc1"][:, 256:384], onesr(128), bufs["ewrow"][0:1, 0:128], name="cEbc")
                c.tt(bufs["csb"][hb:hb + 64, 0:128], ctv, bufs["pc1"][hb:hb + 64, 256:384], ALU.mult, name="cCs")
                xtv = View(bufs["xtok"], 0, 128, t4 * 256 + 64 * h, t4 * 256 + 64 * h + 64,
                           ap=bufs["xtok"].h[:, t4 * 256 + 64 * h:t4 * 256 + 64 * h + 64])
                c.mm(bufs["pcy"][hb:hb + 64, 0:128], xtv, bufs["Rsb"][:, 0:128], start=True, stop=False, name="cYi")
                ssb = View(bufs["SSb"], hb, hb + 64, hp * 64, hp * 64 + 64,
                           ap=bufs["SSb"].h[hb:hb + 64, hp * 64:hp * 64 + 64])
                c.mm(bufs["pcy"][hb:hb + 64, 0:128], ssb, bufs["csb"][hb:hb + 64, 0:128], start=False, stop=True, name="cYs")
                c.tr(bufs["pcb"][hb:hb + 64, 128:256], xtv, bufs["identb"][slice(0, 128), slice(0, 128)], name="cXtr")
                c.cp(bufs["xTd"][hb:hb + 64, 0:128], bufs["pcb"][hb:hb + 64, 128:256], name="cXtrev")
                ztv = View(bufs["zT"], hb, hb + 64, hp * 512 + t4 * 128, hp * 512 + t4 * 128 + 128,
                           ap=bufs["zT"].h[hb:hb + 64, hp * 512 + t4 * 128:hp * 512 + t4 * 128 + 128])
                c.act(bufs["silz"][hb:hb + 64, 0:128], ztv, AF.Silu, name="cSz")
                c.ts(bufs["t64"][hb:hb + 64, 0:128], bufs["xTd"][hb:hb + 64, 0:128],
                     bufs["DhS"][slice(hb, hb + 64), slice(hp, hp + 1)], ALU.mult, name="cDx")
                c.tt(bufs["t64"][hb:hb + 64, 0:128], bufs["t64"][hb:hb + 64, 0:128],
                     bufs["pcy"][hb:hb + 64, 0:128], ALU.add, name="cYa")
                yzv = View(bufs["yzT"], hb, hb + 64, hp * 128, hp * 128 + 128,
                           ap=bufs["yzT"].h[hb:hb + 64, hp * 128:hp * 128 + 128])
                p.op("DVE", lambda e, yzv=yzv, b=bufs, hb=hb: e.tensor_tensor(
                         yzv.ap, b["t64"].h[hb:hb + 64, 0:128],
                         b["silz"].h[hb:hb + 64, 0:128], ALU.mult),
                     reads=[bufs["t64"][hb:hb + 64, 0:128], bufs["silz"][hb:hb + 64, 0:128]],
                     writes=[yzv], name="cYz")
                ov = View(bufs["yzO"], 64 * hl, 64 * hl + 64, hp * T + ch * 128, hp * T + ch * 128 + 128,
                          ap=bufs["yzO"].h[64 * hl:64 * hl + 64, hp * T + ch * 128:hp * T + ch * 128 + 128])
                p.dma("SP", ov, yzv, name="syz")
                c.ts(bufs["bsc"][:, 0:64], btv, bufs["ewsc"][:, 0:1], ALU.mult, name="cBsc")
                c.mm(bufs["pcy"][hb:hb + 64, 128:192], bufs["bsc"][:, 0:64], xtv, name="cSu")
                ssf = View(bufs["SSf"], hb, hb + 64, hp * 64, hp * 64 + 64,
                           ap=bufs["SSf"].h[hb:hb + 64, hp * 64:hp * 64 + 64])
                p.op("DVE", lambda e, ssf=ssf, b=bufs, hb=hb, hp=hp: e.scalar_tensor_tensor(
                         ssf.ap, ssf.ap, b["ewlb2"].h[hb:hb + 64, 0:1],
                         b["pcy"].h[hb:hb + 64, 128:192], ALU.mult, ALU.add),
                     reads=[ssf, bufs["ewlb2"][hb:hb + 64, 0:1], bufs["pcy"][hb:hb + 64, 128:192]],
                     writes=[ssf], name="cSup")
                c.cp(View(bufs["SSb"], hb, hb + 64, hp * 64, hp * 64 + 64,
                          ap=bufs["SSb"].h[hb:hb + 64, hp * 64:hp * 64 + 64]), ssf, name="cSbf")
    p.barrier("fin")
    nops = p.emit()
    for cm in reversed(stack):
        cm.__exit__(None, None, None)
    return nops


def prep_C_inputs(inputs, core, x2_full_T):
    f = np.float32
    import ml_dtypes
    d = {"x2F": x2_full_T, "cstC": host_consts(),
         "identb": np.eye(128, dtype=ml_dtypes.bfloat16)}
    Wi = np.asarray(inputs["L1_attn_W_in"], f)
    DI = 2048
    HN = 2048
    h0 = HPC * core
    Wzc = np.zeros((DM, 512), f)
    for hp in range(2):
        for hl in range(2):
            h = 2 * hp + hl
            Wzc[:, hp * 128 + hl * 64: hp * 128 + hl * 64 + 64] = Wi[DI + 64 * (h0 + h):DI + 64 * (h0 + h) + 64].T
            Wzc[:, 256 + hp * 128 + hl * 64: 256 + hp * 128 + hl * 64 + 64] = \
                Wi[2 * DI + HN + 64 * (h0 + h):2 * DI + HN + 64 * (h0 + h) + 64].T
    d["Wzc"] = Wzc
    Wxba = np.zeros((DM, 516), f)
    for h in range(4):
        Wxba[:, 64 * h:64 * h + 64] = Wi[64 * (h0 + h):64 * (h0 + h) + 64].T
        Wxba[:, 256 + 64 * h:256 + 64 * h + 64] = Wi[2 * DI + 64 * (h0 + h):2 * DI + 64 * (h0 + h) + 64].T
        Wxba[:, 512 + h] = Wi[2 * DI + 2 * HN + h0 + h]
    d["Wxba"] = Wxba
    Dh = np.asarray(inputs["L1_attn_D"], f)
    D2 = np.zeros((128, 2), f)
    for hp in range(2):
        for hl in range(2):
            D2[64 * hl:64 * hl + 64, hp] = Dh[h0 + 2 * hp + hl]
    d["Dh"] = D2
    return d



def build_phaseD(nc):
    NT = TPC
    p = Prog(nc)
    c = Ctx(p)
    stack = []
    bufs = {}
    mk = lambda nm, shape, dt, sp='S': bufs.__setitem__(nm, mkbuf(p, nc, stack, "s_" + nm, shape, dt, sp))
    bufs["yzD"] = dbuf(nc, "yzD", (DM, NT), BF16, kind="ExternalInput")
    bufs["x2D"] = dbuf(nc, "x2D", (DM, NT), F32, kind="ExternalInput")
    bufs["gbD"] = dbuf(nc, "gbD", (128, 32), F32, kind="ExternalInput")
    bufs["WoutT"] = dbuf(nc, "WoutT", (DM, DM), F32, kind="ExternalInput")
    bufs["gateT"] = dbuf(nc, "gateT", (DM, 17), F32, kind="ExternalInput")
    bufs["sgT"] = dbuf(nc, "sgT", (DM, 1), F32, kind="ExternalInput")
    bufs["sguT"] = dbuf(nc, "sguT", (DM, 11264), F32, kind="ExternalInput")
    bufs["sdT"] = dbuf(nc, "sdT", (5632, DM), F32, kind="ExternalInput")
    bufs["w13T"] = dbuf(nc, "w13T", (16 * DM, 2816), F32, kind="ExternalInput")
    bufs["w2T"] = dbuf(nc, "w2T", (16 * 1408, DM), F32, kind="ExternalInput")
    bufs["cstD"] = dbuf(nc, "cstDD", (128, CST_W), F32, kind="ExternalInput")
    bufs["x4T"] = dbuf(nc, "x4T", (DM, NT), F32, kind="ExternalOutput")
    bufs["x3sav"] = dbuf(nc, "x3sav", (DM, NT), F32, kind="Internal")
    bufs["x3bsav"] = dbuf(nc, "x3bsav", (DM, NT), BF16, kind="Internal")
    bufs["coefD"] = dbuf(nc, "coefD", (16, NT), F32, kind="Internal")

    mk("cstS", (128, CST_W), F32)
    mk("lnsq", (128, NT), F32)
    for nm in ("lnsum", "lnssq", "lnmu", "lnvar", "lnt", "lnrs", "sumsqx"):
        mk(nm, (1, NT), F32)
    mk("lnmub", (128, NT), F32)
    mk("lnrsb", (128, NT), F32)
    for nm in ("pp0", "pp3"):
        mk(nm, (128, 512), F32, 'P')
    mk("ppw", (128, 1024), F32, 'P')
    mk("ppg", (128, 1024), F32, 'P')
    mk("ppu", (128, 1024), F32, 'P')
    onesr = lambda w: bufs["cstS"][0:1, CST["ones"][0]:CST["ones"][0] + w]
    onescol = bufs["cstS"][slice(0, 128), CST["ones"][0]:CST["ones"][0] + 1]
    p.dma("SP", bufs["cstS"].full, bufs["cstD"].full, name="lcst")
    xch = lambda buf, ic, w=NT: bufs[buf][slice(0, 128), slice(ic * w, ic * w + w)]

    stackA = []
    mkA = lambda nm, shape, dt, sp='S': bufs.__setitem__(nm, mkbuf(p, nc, stackA, "s_" + nm, shape, dt, sp))
    mkA("xT", (128, 16 * NT), F32)
    mkA("onebS", (128, 1), BF16)
    mkA("yzS", (128, 16 * NT), BF16)
    mkA("gbS", (128, 32), F32)
    mkA("WoS", (128, 2 * 16 * 128), BF16)
    p.dma("SP", bufs["gbS"].full, bufs["gbD"].full, name="lgb")
    c.memset(bufs["onebS"].full, 1.0, name="oneb")
    yz3 = bufs["yzD"].h.rearrange("(dc p) t -> p dc t", p=128)
    p.dma("SP", View(bufs["yzS"], 0, 128, 0, 16 * NT,
                     ap=bufs["yzS"].h.rearrange("p (dc t) -> p dc t", dc=16)),
          View(bufs["yzD"], 0, 1 << 30, 0, 1 << 30, ap=yz3), name="lyz")
    x23 = bufs["x2D"].h.rearrange("(dc p) t -> p dc t", p=128)
    p.dma("SP", View(bufs["xT"], 0, 128, 0, 16 * NT,
                     ap=bufs["xT"].h.rearrange("p (dc t) -> p dc t", dc=16)),
          View(bufs["x2D"], 0, 1 << 30, 0, 1 << 30, ap=x23), name="lx2")
    for ic in range(16):
        for th in range(2):
            c.mm(bufs["ppw"][0:1, th * 512:th * 512 + 512], bufs["onebS"][slice(0, 128), slice(0, 1)],
                 View(bufs["yzS"], 0, 128, ic * NT + th * 512, ic * NT + th * 512 + 512,
                      ap=bufs["yzS"].h[:, ic * NT + th * 512:ic * NT + th * 512 + 512]),
                 start=(ic == 0), stop=(ic == 15), name="gsum")
    c.cp(bufs["lnsum"][0:1, 0:NT], bufs["ppw"][0:1, 0:NT], name="gsumev")
    for ic in range(16):
        c.act(bufs["lnsq"][:, 0:NT], xch("yzS", ic), AF.Square, name="gsq")
        for th in range(2):
            c.mm(bufs["ppw"][0:1, th * 512:th * 512 + 512], onescol,
                 bufs["lnsq"][:, th * 512:th * 512 + 512], start=(ic == 0), stop=(ic == 15), name="gssq")
    c.cp(bufs["lnssq"][0:1, 0:NT], bufs["ppw"][0:1, 0:NT], name="gssqev")
    c.ts(bufs["lnmu"][0:1, 0:NT], bufs["lnsum"][0:1, 0:NT], 1.0 / DM, ALU.mult, name="gmu")
    c.tt(bufs["lnt"][0:1, 0:NT], bufs["lnmu"][0:1, 0:NT], bufs["lnmu"][0:1, 0:NT], ALU.mult, name="gmu2")
    c.stt(bufs["lnvar"][0:1, 0:NT], bufs["lnssq"][0:1, 0:NT], 1.0 / DM, bufs["lnt"][0:1, 0:NT],
          ALU.mult, ALU.subtract, name="gvar")
    c.act(bufs["lnt"][0:1, 0:NT], bufs["lnvar"][0:1, 0:NT], AF.Sqrt,
          bias=_c(bufs, "c1e5", slice(0, 1)), name="gsd")
    c.recip(bufs["lnrs"][0:1, 0:NT], bufs["lnt"][0:1, 0:NT], name="grs")
    for th in range(2):
        c.mm(bufs["ppw"][:, th * 512:th * 512 + 512], onesr(128), bufs["lnmu"][0:1, th * 512:th * 512 + 512], name="gbmu")
    c.cp(bufs["lnmub"][:, 0:NT], bufs["ppw"][:, 0:NT], name="gbmuev")
    for th in range(2):
        c.mm(bufs["ppw"][:, th * 512:th * 512 + 512], onesr(128), bufs["lnrs"][0:1, th * 512:th * 512 + 512], name="gbrs")
    c.cp(bufs["lnrsb"][:, 0:NT], bufs["ppw"][:, 0:NT], name="gbrsev")
    for ic in range(16):
        c.tt(bufs["lnsq"][:, 0:NT], xch("yzS", ic), bufs["lnmub"][:, 0:NT], ALU.subtract, name="gn1")
        c.tt(bufs["lnsq"][:, 0:NT], bufs["lnsq"][:, 0:NT], bufs["lnrsb"][:, 0:NT], ALU.mult, name="gn2")
        c.ts(xch("yzS", ic), bufs["lnsq"][:, 0:NT], bufs["gbS"][slice(0, 128), slice(ic, ic + 1)],
             ALU.mult, bufs["gbS"][slice(0, 128), slice(16 + ic, 16 + ic + 1)], ALU.add, name="gn3")
    wsrc3 = bufs["WoutT"].h.rearrange("(dc p) m -> p dc m", p=128)
    wdst3 = bufs["WoS"].h.rearrange("p (r dc m) -> p r dc m", r=2, dc=16)
    for ec in range(16):
        rr = ec % 2
        p.dma("POOL", View(bufs["WoS"], 0, 128, rr * 2048, rr * 2048 + 2048, ap=wdst3[:, rr]),
              View(bufs["WoutT"], 0, 1 << 30, 0, 1 << 30,
                   ap=wsrc3[:, :, ec * 128:(ec + 1) * 128]), name="lwo")
        for th in range(2):
            for dc2 in range(16):
                c.mm(bufs["ppw"][:, 0:512], bufs["WoS"][:, rr * 2048 + dc2 * 128:rr * 2048 + dc2 * 128 + 128],
                     View(bufs["yzS"], 0, 128, dc2 * NT + th * 512, dc2 * NT + th * 512 + 512,
                          ap=bufs["yzS"].h[:, dc2 * NT + th * 512:dc2 * NT + th * 512 + 512]),
                     start=(dc2 == 0), stop=(dc2 == 15), name="wmm")
            xtv = View(bufs["xT"], 0, 128, ec * NT + th * 512, ec * NT + th * 512 + 512,
                       ap=bufs["xT"].h[:, ec * NT + th * 512:ec * NT + th * 512 + 512])
            c.tt(xtv, xtv, bufs["ppw"][:, 0:512], ALU.add, name="wres")
    _ln_emit(c, bufs, [xch("xT", ic) for ic in range(16)],
             [xch("xT", ic) for ic in range(16)], None, 16, NT, 1e-5, "l3")
    for ic in range(16):
        p.dma("SP", View(bufs["x3sav"], 0, 1 << 30, 0, 1 << 30,
                         ap=bufs["x3sav"].h[ic * 128:(ic + 1) * 128, 0:NT]),
              xch("xT", ic), name="sx3")
        p.dma("POOL", View(bufs["x3bsav"], 0, 1 << 30, 0, 1 << 30,
                           ap=bufs["x3bsav"].h[ic * 128:(ic + 1) * 128, 0:NT]),
              xch("xT", ic), name="sx3b")
    p.barrier("p1end")
    for cm in reversed(stackA):
        cm.__exit__(None, None, None)
    mk("x3b", (128, 16 * NT), BF16)
    mk("accS", (128, 16 * NT), F32)
    mk("hS", (128, 6 * NT), BF16)
    mk("wgS", (128, 2 * 16 * 128), BF16)
    mk("wuS", (128, 2 * 16 * 128), BF16)
    mk("w2S", (128, 6 * 16 * 128), BF16)
    mk("rmx", (128, 1), F32)
    mk("rsum", (128, 1), F32)
    mk("sigS", (128, 8), F32)
    mk("crow", (1, NT), F32)
    mk("cbc", (128, NT), F32)
    mk("silS", (128, NT), F32)
    x3b3 = bufs["x3bsav"].h.rearrange("(dc p) t -> p dc t", p=128)
    p.dma("SP", View(bufs["x3b"], 0, 128, 0, 16 * NT,
                     ap=bufs["x3b"].h.rearrange("p (dc t) -> p dc t", dc=16)),
          View(bufs["x3bsav"], 0, 1 << 30, 0, 1 << 30, ap=x3b3), name="lx3b")
    gt3 = bufs["gateT"].h.rearrange("(dc p) m -> p dc m", p=128)
    mk("gwS", (128, 16 * 17), BF16)
    gsrc = np.zeros(0)
    p.dma("POOL", View(bufs["gwS"], 0, 128, 0, 16 * 17,
                       ap=bufs["gwS"].h.rearrange("p (dc m) -> p dc m", dc=16)),
          View(bufs["gateT"], 0, 1 << 30, 0, 1 << 30,
               ap=bufs["gateT"].h.rearrange("(dc p) m -> p dc m", p=128)), name="lgw")
    for tt8 in range(8):
        for dc in range(16):
            c.mm(bufs["pp0"][:, 0:17], bufs["x3b"][:, dc * NT + tt8 * 128:dc * NT + tt8 * 128 + 128],
                 bufs["gwS"][:, dc * 17:dc * 17 + 17], start=(dc == 0), stop=(dc == 15), name="rlog")
        rw = View(bufs["lnrsb"], 0, 128, tt8 * 16, tt8 * 16 + 16,
                  ap=bufs["lnrsb"].h[:, tt8 * 16:tt8 * 16 + 16])
        c.red(bufs["rmx"][:, 0:1], bufs["pp0"][:, 0:16], ALU.max, name="rmax")
        c.ts(bufs["rmx"][:, 0:1], bufs["rmx"][:, 0:1], -1.0, ALU.mult, name="rneg")
        p.op("ACT", lambda e, rw=rw, b=bufs: e.activation(rw.ap, b["pp0"].h[:, 0:16], AF.Exp,
                                                          bias=b["rmx"].h[:, 0:1]),
             reads=[bufs["pp0"][slice(0, 128), slice(0, 16)], bufs["rmx"][:, 0:1]],
             writes=[rw], name="rexp")
        c.red(bufs["rsum"][:, 0:1], rw, ALU.add, name="rsum")
        c.recip(bufs["rsum"][:, 0:1], bufs["rsum"][:, 0:1], name="rrec")
        c.ts(rw, rw, bufs["rsum"][:, 0:1], ALU.mult, name="rnorm")
        c.act(bufs["sigS"][:, tt8:tt8 + 1], bufs["pp0"][:, 16:17], AF.Sigmoid, name="rsig")
        t_ = bufs["lnsq"]
        c.cp(t_[:, 0:16], rw, name="tau0")
        for it in range(3):
            c.red(bufs["rmx"][:, 0:1], t_[:, 0:16], ALU.max, name="taum")
            c.ts(t_[:, 16:32], t_[:, 0:16], bufs["rmx"][:, 0:1], ALU.is_ge, name="taug")
            c.ts(t_[:, 16:32], t_[:, 16:32], 1e30, ALU.mult, name="taus")
            c.tt(t_[:, 0:16], t_[:, 0:16], t_[:, 16:32], ALU.subtract, name="tausb")
        c.red(bufs["rmx"][:, 0:1], t_[:, 0:16], ALU.max, name="tauf")
        c.ts(t_[:, 0:16], rw, bufs["rmx"][:, 0:1], ALU.is_ge, name="cof1")
        c.tt(t_[:, 0:16], t_[:, 0:16], rw, ALU.mult, name="cof2")
        for e_ in range(16):
            p.dma("SP", View(bufs["coefD"], 0, 1 << 30, 0, 1 << 30,
                             ap=bufs["coefD"].h[e_:e_ + 1, tt8 * 128:tt8 * 128 + 128]),
                  t_[:, e_:e_ + 1], name="scoef")
    c.memset(bufs["accS"].full, 0.0)
    sg3 = bufs["sguT"].h.rearrange("(dc p) m -> p dc m", p=128)
    alljt = list(range(44))
    for g0 in range(0, 44, 6):
        sub = alljt[g0:g0 + 6]
        for si, jg in enumerate(sub):
            p.dma("POOL", View(bufs["wgS"], 0, 128, (si % 2) * 2048, (si % 2) * 2048 + 2048,
                               ap=bufs["wgS"].h.rearrange("p (r dc m) -> p r dc m", r=2, dc=16)[:, si % 2]),
                  View(bufs["sguT"], 0, 1 << 30, 0, 1 << 30,
                       ap=sg3[:, :, jg * 128:jg * 128 + 128]), name="lsg")
            p.dma("POOL", View(bufs["wuS"], 0, 128, (si % 2) * 2048, (si % 2) * 2048 + 2048,
                               ap=bufs["wuS"].h.rearrange("p (r dc m) -> p r dc m", r=2, dc=16)[:, si % 2]),
                  View(bufs["sguT"], 0, 1 << 30, 0, 1 << 30,
                       ap=sg3[:, :, 5632 + jg * 128:5632 + jg * 128 + 128]), name="lsu")
            p.dma("POOL", View(bufs["w2S"], 0, 128, si * 2048, si * 2048 + 2048,
                               ap=bufs["w2S"].h.rearrange("p (j dc m) -> p j dc m", j=6, dc=16)[:, si]),
                  View(bufs["sdT"], 0, 1 << 30, 0, 1 << 30,
                       ap=bufs["sdT"].h.rearrange("(jc p) m -> p jc m", p=128)[:, jg]), name="lsd")
            for th in range(2):
                for dc in range(16):
                    c.mm(bufs["ppg"][:, th * 512:th * 512 + 512],
                         bufs["wgS"][:, (si % 2) * 2048 + dc * 128:(si % 2) * 2048 + dc * 128 + 128],
                         View(bufs["x3b"], 0, 128, dc * NT + th * 512, dc * NT + th * 512 + 512,
                              ap=bufs["x3b"].h[:, dc * NT + th * 512:dc * NT + th * 512 + 512]),
                         start=(dc == 0), stop=(dc == 15), name="sgmm")
                    c.mm(bufs["ppu"][:, th * 512:th * 512 + 512],
                         bufs["wuS"][:, (si % 2) * 2048 + dc * 128:(si % 2) * 2048 + dc * 128 + 128],
                         View(bufs["x3b"], 0, 128, dc * NT + th * 512, dc * NT + th * 512 + 512,
                              ap=bufs["x3b"].h[:, dc * NT + th * 512:dc * NT + th * 512 + 512]),
                         start=(dc == 0), stop=(dc == 15), name="summ")
            c.act(bufs["silS"][:, 0:NT], bufs["ppg"][:, 0:NT], AF.Silu, name="ssil")
            c.tt(View(bufs["hS"], 0, 128, si * NT, si * NT + NT,
                      ap=bufs["hS"].h[:, si * NT:si * NT + NT]),
                 bufs["silS"][:, 0:NT], bufs["ppu"][:, 0:NT], ALU.mult, name="sh")
        for ec in range(16):
            for th in range(2):
                for si in range(len(sub)):
                    c.mm(bufs["ppw"][:, 0:512],
                         bufs["w2S"][:, si * 2048 + ec * 128:si * 2048 + ec * 128 + 128],
                         View(bufs["hS"], 0, 128, si * NT + th * 512, si * NT + th * 512 + 512,
                              ap=bufs["hS"].h[:, si * NT + th * 512:si * NT + th * 512 + 512]),
                         start=(si == 0), stop=(si == len(sub) - 1), name="sdmm")
                av = View(bufs["accS"], 0, 128, ec * NT + th * 512, ec * NT + th * 512 + 512,
                          ap=bufs["accS"].h[:, ec * NT + th * 512:ec * NT + th * 512 + 512])
                c.tt(av, av, bufs["ppw"][:, 0:512], ALU.add, name="sacc")
    for tt8 in range(8):
        c.tr(bufs["pp3"][0:1, 0:128], bufs["sigS"][:, tt8:tt8 + 1], _c(bufs, "ident"), name="sgtr")
        c.cp(View(bufs["crow"], 0, 1, tt8 * 128, tt8 * 128 + 128,
                  ap=bufs["crow"].h[0:1, tt8 * 128:tt8 * 128 + 128]), bufs["pp3"][0:1, 0:128], name="sgtev")
    for th in range(2):
        c.mm(bufs["ppw"][:, th * 512:th * 512 + 512], onesr(128),
             bufs["crow"][0:1, th * 512:th * 512 + 512], name="sgbc")
    c.cp(bufs["cbc"][:, 0:NT], bufs["ppw"][:, 0:NT], name="sgbcev")
    for ic in range(16):
        c.tt(xch("accS", ic), xch("accS", ic), bufs["cbc"][:, 0:NT], ALU.mult, name="sgmul")
    w13r = bufs["w13T"].h.rearrange("(e dc p) m -> p e dc m", e=16, p=128)
    w2r = bufs["w2T"].h.rearrange("(e jc p) m -> p e jc m", e=16, p=128)
    for e_ in range(16):
        p.dma("SP", bufs["crow"][0:1, 0:NT],
              View(bufs["coefD"], 0, 1 << 30, 0, 1 << 30, ap=bufs["coefD"].h[e_:e_ + 1, 0:NT]), name="lcoef")
        for th in range(2):
            c.mm(bufs["ppw"][:, th * 512:th * 512 + 512], onesr(128),
                 bufs["crow"][0:1, th * 512:th * 512 + 512], name="cbcmm")
        c.cp(bufs["cbc"][:, 0:NT], bufs["ppw"][:, 0:NT], name="cbcev")
        for g0 in range(0, 11, 6):
            sub = list(range(11))[g0:g0 + 6]
            for si, jt in enumerate(sub):
                p.dma("POOL", View(bufs["wgS"], 0, 128, (si % 2) * 2048, (si % 2) * 2048 + 2048,
                                   ap=bufs["wgS"].h.rearrange("p (r dc m) -> p r dc m", r=2, dc=16)[:, si % 2]),
                      View(bufs["w13T"], 0, 1 << 30, 0, 1 << 30,
                           ap=w13r[:, e_, :, jt * 128:jt * 128 + 128]), name="lg13")
                p.dma("POOL", View(bufs["wuS"], 0, 128, (si % 2) * 2048, (si % 2) * 2048 + 2048,
                                   ap=bufs["wuS"].h.rearrange("p (r dc m) -> p r dc m", r=2, dc=16)[:, si % 2]),
                      View(bufs["w13T"], 0, 1 << 30, 0, 1 << 30,
                           ap=w13r[:, e_, :, 1408 + jt * 128:1408 + jt * 128 + 128]), name="lu13")
                p.dma("POOL", View(bufs["w2S"], 0, 128, si * 2048, si * 2048 + 2048,
                                   ap=bufs["w2S"].h.rearrange("p (j dc m) -> p j dc m", j=6, dc=16)[:, si]),
                      View(bufs["w2T"], 0, 1 << 30, 0, 1 << 30, ap=w2r[:, e_, jt]), name="lw2e")
                for th in range(2):
                    for dc in range(16):
                        c.mm(bufs["ppg"][:, th * 512:th * 512 + 512],
                             bufs["wgS"][:, (si % 2) * 2048 + dc * 128:(si % 2) * 2048 + dc * 128 + 128],
                             View(bufs["x3b"], 0, 128, dc * NT + th * 512, dc * NT + th * 512 + 512,
                                  ap=bufs["x3b"].h[:, dc * NT + th * 512:dc * NT + th * 512 + 512]),
                             start=(dc == 0), stop=(dc == 15), name="egmm")
                        c.mm(bufs["ppu"][:, th * 512:th * 512 + 512],
                             bufs["wuS"][:, (si % 2) * 2048 + dc * 128:(si % 2) * 2048 + dc * 128 + 128],
                             View(bufs["x3b"], 0, 128, dc * NT + th * 512, dc * NT + th * 512 + 512,
                                  ap=bufs["x3b"].h[:, dc * NT + th * 512:dc * NT + th * 512 + 512]),
                             start=(dc == 0), stop=(dc == 15), name="eumm")
                c.act(bufs["silS"][:, 0:NT], bufs["ppg"][:, 0:NT], AF.Silu, name="esil")
                c.tt(bufs["silS"][:, 0:NT], bufs["silS"][:, 0:NT], bufs["ppu"][:, 0:NT], ALU.mult, name="ehu")
                c.tt(View(bufs["hS"], 0, 128, si * NT, si * NT + NT,
                          ap=bufs["hS"].h[:, si * NT:si * NT + NT]),
                     bufs["silS"][:, 0:NT], bufs["cbc"][:, 0:NT], ALU.mult, name="ehc")
            for ec in range(16):
                for th in range(2):
                    for si in range(len(sub)):
                        c.mm(bufs["ppw"][:, 0:512],
                             bufs["w2S"][:, si * 2048 + ec * 128:si * 2048 + ec * 128 + 128],
                             View(bufs["hS"], 0, 128, si * NT + th * 512, si * NT + th * 512 + 512,
                                  ap=bufs["hS"].h[:, si * NT + th * 512:si * NT + th * 512 + 512]),
                             start=(si == 0), stop=(si == len(sub) - 1), name="edmm")
                    av = View(bufs["accS"], 0, 128, ec * NT + th * 512, ec * NT + th * 512 + 512,
                              ap=bufs["accS"].h[:, ec * NT + th * 512:ec * NT + th * 512 + 512])
                    c.tt(av, av, bufs["ppw"][:, 0:512], ALU.add, name="eacc")
    for ic in range(16):
        p.dma("SP", bufs["lnmub"][:, 0:NT],
              View(bufs["x3sav"], 0, 1 << 30, 0, 1 << 30,
                   ap=bufs["x3sav"].h[ic * 128:(ic + 1) * 128, 0:NT]), name="lx3f")
        c.tt(xch("accS", ic), xch("accS", ic), bufs["lnmub"][:, 0:NT], ALU.add, name="fres")
    _ln_emit(c, bufs, [xch("accS", ic) for ic in range(16)],
             [xch("accS", ic) for ic in range(16)], None, 16, NT, 1e-5, "lf")
    for ic in range(16):
        p.dma("SP", View(bufs["x4T"], 0, 1 << 30, 0, 1 << 30,
                         ap=bufs["x4T"].h[ic * 128:(ic + 1) * 128, 0:NT]),
              xch("accS", ic), name="sx4")
    p.barrier("fin")
    nops = p.emit()
    for cm in reversed(stack):
        cm.__exit__(None, None, None)
    return nops


def prep_D_inputs(inputs, core, yz_full, x2_full_T):
    f = np.float32
    t0 = core * TPC
    d = {}
    d["yzD"] = np.ascontiguousarray(yz_full[:, t0:t0 + TPC])
    d["x2D"] = np.ascontiguousarray(x2_full_T[:, t0:t0 + TPC])
    gb = np.zeros((128, 32), f)
    gb[:, 0:16] = np.asarray(inputs["L1_attn_gamma"], f).reshape(16, 128).T
    gb[:, 16:32] = np.asarray(inputs["L1_attn_beta"], f).reshape(16, 128).T
    d["gbD"] = gb
    d["WoutT"] = np.ascontiguousarray(np.asarray(inputs["L1_attn_W_out"], f).T)
    gw = np.zeros((DM, 17), f)
    gw[:, 0:16] = np.asarray(inputs["L1_ffn_gate_w"], f).T
    gw[:, 16] = np.asarray(inputs["L1_ffn_sgate_w"], f)[0]
    d["gateT"] = gw
    d["sgT"] = np.zeros((DM, 1), f)
    d["sguT"] = np.ascontiguousarray(np.asarray(inputs["L1_ffn_shared_gate_up_w"], f).T)
    d["sdT"] = np.ascontiguousarray(np.asarray(inputs["L1_ffn_shared_down_w"], f).T)
    d["w13T"] = np.ascontiguousarray(
        np.asarray(inputs["L1_ffn_w13"], f).transpose(0, 2, 1).reshape(16 * DM, 2816))
    d["w2T"] = np.ascontiguousarray(
        np.asarray(inputs["L1_ffn_w2"], f).transpose(0, 2, 1).reshape(16 * 1408, DM))
    d["cstDD"] = host_consts()
    return d


_CACHE = {}


def _get(nm, builder):
    if nm not in _CACHE:
        nc = bass.Bass("TRN2", target_bir_lowering=False)
        builder(nc)
        _CACHE[nm] = nc
    return _CACHE[nm]


def kernel(**inputs):
    if FUSED:
        return kernel_fused(**inputs)
    f = np.float32
    ncA = _get("A", lambda nc: build_phaseA(nc, T))
    in_maps = [prep_A_inputs(inputs, core, T) for core in range(NCORE)]
    resA = run_bass_kernel_spmd(ncA, in_maps, core_ids=list(range(NCORE)))
    oT_full = np.concatenate([resA.results[k]["oheads"] for k in range(NCORE)], axis=0)
    ncB = _get("B", build_phaseB)
    in_maps = [prep_B_inputs(inputs, core, oT_full) for core in range(NCORE)]
    resB = run_bass_kernel_spmd(ncB, in_maps, core_ids=list(range(NCORE)))
    x2_full_T = np.ascontiguousarray(
        np.concatenate([resB.results[k]["x2T"] for k in range(NCORE)], axis=1))
    ncC = _get("C", build_phaseC)
    in_maps = [prep_C_inputs(inputs, core, x2_full_T) for core in range(NCORE)]
    resC = run_bass_kernel_spmd(ncC, in_maps, core_ids=list(range(NCORE)))
    import ml_dtypes
    yz_full = np.zeros((DM, T), ml_dtypes.bfloat16)
    for k in range(NCORE):
        yzo = resC.results[k]["yzO"]
        for h in range(HPC):
            hp, hl = h // 2, h % 2
            gh = HPC * k + h
            yz_full[64 * gh:64 * gh + 64, :] = yzo[64 * hl:64 * hl + 64, hp * T:(hp + 1) * T]
    ncD = _get("D", build_phaseD)
    in_maps = [prep_D_inputs(inputs, core, yz_full, x2_full_T) for core in range(NCORE)]
    resD = run_bass_kernel_spmd(ncD, in_maps, core_ids=list(range(NCORE)))
    x4 = np.concatenate([resD.results[k]["x4T"] for k in range(NCORE)], axis=1)
    return np.ascontiguousarray(x4.T).reshape(1, T, DM).astype(f)


def emit_B_ffn(c, bufs, NT):
    p = c.p
    onesr = lambda w: bufs["cstS"][0:1, CST["ones"][0]:CST["ones"][0] + w]
    onescol = bufs["cstS"][slice(0, 128), CST["ones"][0]:CST["ones"][0] + 1]
    xch = lambda buf, ic, w=NT: bufs[buf][slice(0, 128), slice(ic * w, ic * w + w)]
    pps = bufs["pps"]
    for ic in range(16):
        c.act(bufs["lnsq"][:, 0:NT], xch("x1n", ic), AF.Square, name="xsq")
        for th in range(2):
            c.mm(bufs["ppw"][0:1, th * 512:th * 512 + 512],
                 bufs["cstS"][slice(0, 128), CST["ones"][0]:CST["ones"][0] + 1],
                 bufs["lnsq"][:, th * 512:th * 512 + 512], start=(ic == 0), stop=(ic == 15), name="xssq")
    c.cp(bufs["sumsqx"][0:1, 0:NT], bufs["ppw"][0:1, 0:NT], name="xssqev")
    pps = bufs["pps"]
    for th in range(2):
        for dc2 in range(16):
            c.mm(pps[0:64, th * 512:(th + 1) * 512] if th == 0 else pps[0:64, 512:1024],
                 bufs["swS"][:, dc2 * 64:(dc2 + 1) * 64],
                 xch("x1b", dc2)[slice(0, 128), slice(th * 512, th * 512 + 512)] if False else
                 View(bufs["x1n"], 0, 128, dc2 * NT + th * 512, dc2 * NT + th * 512 + 512,
                      ap=bufs["x1n"].h[:, dc2 * NT + th * 512:dc2 * NT + th * 512 + 512]),
                 start=(dc2 == 0), stop=(dc2 == 15), name="swmm")
    for dc2 in range(16):
        c.mm(bufs["pp3"][0:64, 0:8], bufs["swS"][:, dc2 * 64:(dc2 + 1) * 64],
             bufs["mdS"][:, dc2 * 8:(dc2 + 1) * 8], start=(dc2 == 0), stop=(dc2 == 15), name="cswmm")
    c.tt(bufs["cswS"][0:64, 0:8], bufs["pp3"][0:64, 0:8], _c(bufs, "dmask", slice(0, 64), 8), ALU.mult, name="cswm")
    c.red(bufs["cswC"][0:64, 0:1], bufs["cswS"][0:64, 0:8], ALU.add, name="cswr")
    c.ts(bufs["lnmub"][0:64, 0:NT], pps[0:64, 0:NT], bufs["cswC"][0:64, 0:1], ALU.subtract, name="numsub")
    c.tt(bufs["lnsq"][0:64, 0:NT], bufs["lnmub"][0:64, 0:NT], bufs["lnmub"][0:64, 0:NT], ALU.mult, name="numsq")
    for th in range(2):
        c.mm(bufs["ppw"][0:8, th * 512:th * 512 + 512], _c(bufs, "dmask", slice(0, 64), 8),
             View(bufs["lnsq"], 0, 64, th * 512, th * 512 + 512,
                  ap=bufs["lnsq"].h[0:64, th * 512:th * 512 + 512]), name="n2mm")
    for th in range(2):
        for dc2 in range(16):
            c.mm(bufs["ppq"][0:8, 0:512],
                 bufs["mdS"][:, dc2 * 8:(dc2 + 1) * 8],
                 View(bufs["x1n"], 0, 128, dc2 * NT + th * 512, dc2 * NT + th * 512 + 512,
                      ap=bufs["x1n"].h[:, dc2 * NT + th * 512:dc2 * NT + th * 512 + 512]),
                 start=(dc2 == 0), stop=(dc2 == 15), name="mdxmm")
        c.cp(View(bufs["lnrsb"], 0, 8, th * 512, th * 512 + 512,
                  ap=bufs["lnrsb"].h[0:8, th * 512:th * 512 + 512]), bufs["ppq"][0:8, 0:512], name="mdxev")
    for dc2 in range(16):
        c.mm(bufs["pp3"][0:8, 8:16], bufs["mdS"][:, dc2 * 8:(dc2 + 1) * 8],
             bufs["mdS"][:, dc2 * 8:(dc2 + 1) * 8], start=(dc2 == 0), stop=(dc2 == 15), name="md2mm")
    c.tt(bufs["md2S"][0:8, 0:8], bufs["pp3"][0:8, 8:16], _c(bufs, "ident", slice(0, 8), 8), ALU.mult, name="md2m")
    c.red(bufs["md2C"][0:8, 0:1], bufs["md2S"][0:8, 0:8], ALU.add, name="md2r")
    for th in range(2):
        c.mm(bufs["ppq"][0:8, 0:512], onesr(8),
             View(bufs["sumsqx"], 0, 1, th * 512, th * 512 + 512,
                  ap=bufs["sumsqx"].h[0:1, th * 512:th * 512 + 512]), name="r2bc")
        c.cp(View(bufs["lnsq"], 0, 8, th * 512, th * 512 + 512,
                  ap=bufs["lnsq"].h[0:8, th * 512:th * 512 + 512]), bufs["ppq"][0:8, 0:512], name="r2bcev")
    c.stt(bufs["r2S"][0:8, 0:NT], bufs["lnrsb"][0:8, 0:NT], -2.0, bufs["lnsq"][0:8, 0:NT], ALU.mult, ALU.add, name="r2a")
    c.ts(bufs["r2S"][0:8, 0:NT], bufs["r2S"][0:8, 0:NT], bufs["md2C"][0:8, 0:1], ALU.add, name="r2b")
    c.recip(bufs["lnsq"][0:8, 0:NT], bufs["r2S"][0:8, 0:NT], name="r2rec")
    c.tt(bufs["r2S"][0:8, 0:NT], bufs["lnsq"][0:8, 0:NT], bufs["ppw"][0:8, 0:NT], ALU.mult, name="scm")
    c.ts(bufs["lnrsb"][0:8, 0:NT], bufs["r2S"][0:8, 0:NT], bufs["impS"][0:8, 0:1], ALU.subtract,
         bufs["impS"][0:8, 1:2], ALU.mult, name="gpre")
    c.act(bufs["lnrsb"][0:8, 0:NT], bufs["lnrsb"][0:8, 0:NT], AF.Sigmoid, name="gsig")
    for tt8 in range(8):
        c.tr(bufs["pp3"][0:8, 16:144], View(bufs["lnrsb"], 0, 8, tt8 * 128, tt8 * 128 + 128,
                                            ap=bufs["lnrsb"].h[0:8, tt8 * 128:tt8 * 128 + 128]),
             _c(bufs, "ident", slice(0, 8), 8), name="gtr") if False else None
        c.tr(bufs["ppq"][:, 0:8], View(bufs["lnrsb"], 0, 8, tt8 * 128, tt8 * 128 + 128,
                                       ap=bufs["lnrsb"].h[0:8, tt8 * 128:tt8 * 128 + 128]),
             _c(bufs, "ident", slice(0, 8), 8), name="gtr2")
        c.cp(bufs["gTt"][:, 0:8], bufs["ppq"][:, 0:8], name="gtev")
        c.red(bufs["selc"][:, tt8:tt8 + 1], bufs["gTt"][:, 0:8], ALU.max, name="selr")
        c.ts(bufs["indt"][:, 0:8], bufs["gTt"][:, 0:8], bufs["selc"][:, tt8:tt8 + 1], ALU.is_ge, name="indg")
        c.tr(bufs["ppq"][0:8, 8:136], bufs["indt"][:, 0:8], _c(bufs, "ident"), name="indtr")
        c.cp(View(bufs["indS"], 0, 8, tt8 * 128, tt8 * 128 + 128,
                  ap=bufs["indS"].h[0:8, tt8 * 128:tt8 * 128 + 128]), bufs["ppq"][0:8, 8:136], name="indev")
        c.tr(bufs["ppq"][0:1, 136:264], bufs["selc"][:, tt8:tt8 + 1], _c(bufs, "ident"), name="seltr")
        c.cp(View(bufs["sumsqx"], 0, 1, tt8 * 128, tt8 * 128 + 128,
                  ap=bufs["sumsqx"].h[0:1, tt8 * 128:tt8 * 128 + 128]), bufs["ppq"][0:1, 136:264], name="selev")
    for th in range(2):
        c.mm(bufs["ppw"][:, th * 512:th * 512 + 512], onesr(128),
             bufs["sumsqx"][0:1, th * 512:th * 512 + 512], name="selbc")
    c.cp(bufs["lnrsb"][:, 0:NT], bufs["ppw"][:, 0:NT], name="selbcev")
    for th in range(2):
        for dc2 in range(16):
            c.mm(pps[0:64, th * 512:th * 512 + 512], bufs["updS"][:, dc2 * 64:(dc2 + 1) * 64],
                 View(bufs["x1n"], 0, 128, dc2 * NT + th * 512, dc2 * NT + th * 512 + 512,
                      ap=bufs["x1n"].h[:, dc2 * NT + th * 512:dc2 * NT + th * 512 + 512]),
                 start=(dc2 == 0), stop=(dc2 == 15), name="xumm")
    for dc2 in range(16):
        c.mm(bufs["pp3"][0:64, 8:16], bufs["updS"][:, dc2 * 64:(dc2 + 1) * 64],
             bufs["dbS"][:, dc2 * 8:(dc2 + 1) * 8], start=(dc2 == 0), stop=(dc2 == 15), name="udbmm")
    c.tt(bufs["cswS"][0:64, 0:8], bufs["pp3"][0:64, 8:16], _c(bufs, "dmask", slice(0, 64), 8), ALU.mult, name="udbm")
    c.red(bufs["udbC"][0:64, 0:1], bufs["cswS"][0:64, 0:8], ALU.add, name="udbr")
    c.ts(bufs["lnmub"][0:64, 0:NT], pps[0:64, 0:NT], bufs["udbC"][0:64, 0:1], ALU.subtract, name="modx")
    c.cp(bufs["r2S"][0:8, 0:NT], bufs["indS"][0:8, 0:NT], name="indf")
    for th in range(2):
        c.mm(bufs["ppq"][0:64, 0:512], _c(bufs, "exp8", slice(0, 8), 64),
             View(bufs["r2S"], 0, 8, th * 512, th * 512 + 512,
                  ap=bufs["r2S"].h[0:8, th * 512:th * 512 + 512]), name="indx")
        c.tt(View(bufs["mxiS"], 0, 64, th * 512, th * 512 + 512,
                  ap=bufs["mxiS"].h[0:64, th * 512:th * 512 + 512]),
             View(bufs["lnmub"], 0, 64, th * 512, th * 512 + 512,
                  ap=bufs["lnmub"].h[0:64, th * 512:th * 512 + 512]),
             bufs["ppq"][0:64, 0:512], ALU.mult, name="mxi")
    for ec in range(16):
        for th in range(2):
            pp = bufs["ppw"]
            c.mm(pp[:, 0:512], View(bufs["dgnS"], 0, 64, ec * 128, ec * 128 + 128,
                                    ap=bufs["dgnS"].h[0:64, ec * 128:ec * 128 + 128]),
                 View(bufs["mxiS"], 0, 64, th * 512, th * 512 + 512,
                      ap=bufs["mxiS"].h[0:64, th * 512:th * 512 + 512]),
                 start=True, stop=False, name="dgmm")
            c.mm(pp[:, 0:512], View(bufs["biasS"], 0, 8, ec * 128, ec * 128 + 128,
                                    ap=bufs["biasS"].h[0:8, ec * 128:ec * 128 + 128]),
                 View(bufs["indS"], 0, 8, th * 512, th * 512 + 512,
                      ap=bufs["indS"].h[0:8, th * 512:th * 512 + 512]),
                 start=False, stop=True, name="bimm")
            x1v = View(bufs["x1n"], 0, 128, ec * NT + th * 512, ec * NT + th * 512 + 512,
                       ap=bufs["x1n"].h[:, ec * NT + th * 512:ec * NT + th * 512 + 512])
            sbv = View(bufs["lnrsb"], 0, 128, th * 512, th * 512 + 512,
                       ap=bufs["lnrsb"].h[:, th * 512:th * 512 + 512])
            tv = bufs["lnsq"][:, 0:512]
            c.tt(tv, pp[:, 0:512], x1v, ALU.subtract, name="fcomb1")
            c.tt(tv, tv, sbv, ALU.mult, name="fcomb2")
            xtv = View(bufs["xT"], 0, 128, ec * NT + th * 512, ec * NT + th * 512 + 512,
                       ap=bufs["xT"].h[:, ec * NT + th * 512:ec * NT + th * 512 + 512])
            c.stt(xtv, x1v, 2.0, tv, ALU.mult, ALU.add, name="fcomb3")



def emit_C_fused(c, bufs, nc, p):
    stackC = []
    mkC = lambda nm, shape, dt, sp='S': bufs.__setitem__(nm, mkbuf(p, nc, stackC, "sC_" + nm, shape, dt, sp))
    mkC("DhS", (128, 2), F32)
    mkC("gbCS", (128, 4), F32)
    mkC("woCS", (128, 2 * DM), BF16)
    mkC("WzcS", (128, 16 * 512), BF16)
    mkC("WxbaS", (128, 16 * 516), BF16)
    mkC("xbfS", (128, 16 * 512), BF16)
    mkC("zT", (128, 2 * 512), BF16)
    mkC("cT", (128, 2 * 512), BF16)
    mkC("xtok", (128, 4 * 256), BF16)
    mkC("btok", (128, 4 * 256), BF16)
    mkC("atok", (128, 4 * 4), F32)
    mkC("yzT", (128, 2 * T), BF16)
    mkC("SSf", (128, 2 * 64), F32)
    mkC("SSb", (128, 2 * 64), BF16)
    mkC("ldc", (128, 1), F32)
    mkC("wcol", (128, 1), F32)
    mkC("wrow2", (1, 128), F32)
    mkC("nwrow", (1, 128), F32)
    mkC("wlb", (128, 1), F32)
    mkC("ewl", (1, 1), F32)
    mkC("ewlb2", (128, 1), F32)
    mkC("ewsc", (128, 1), F32)
    mkC("ewrow", (1, 128), F32)
    mkC("Dxp", (128, 128), F32)
    mkC("Rsb", (128, 128), BF16)
    mkC("bTd", (128, 128), BF16)
    mkC("csb", (128, 128), BF16)
    mkC("bsc", (128, 64), BF16)
    mkC("silz", (128, 128), F32)
    mkC("xTd", (128, 128), F32)
    mkC("t64", (128, 128), F32)
    mkC("tot", (1, 1), F32)
    mkC("scr0", (128, 512), F32)
    mkC("scr1", (128, 512), F32)
    mkC("rowa", (1, 512), F32)
    mkC("rowb", (1, 512), F32)
    mkC("rowc", (1, 512), F32)
    mkC("wrowC", (128, DM), BF16)
    pstackC = []
    for nm, shape in (("pc0", (128, 512)), ("pc1", (128, 512)), ("pc2", (128, 512)),
                      ("pcy", (128, 512)), ("pcs", (128, 512))):
        bufs[nm] = mkbuf(p, nc, pstackC, "pC_" + nm, shape, F32, 'P')
    bufs["pcb"] = mkbuf(p, nc, pstackC, "pC_pcb", (128, 256), BF16, 'P')
    onesr = lambda w: bufs["cstS"][0:1, CST["ones"][0]:CST["ones"][0] + w]
    onescol = bufs["cstS"][slice(0, 128), CST["ones"][0]:CST["ones"][0] + 1]
    ident = _c(bufs, "ident")
    p.dma("SP", bufs["DhS"].full, bufs["Dh"].full, name="lDhF")
    p.dma("SP", bufs["gbCS"].full, bufs["gbC"].full, name="lgbC")
    p.dma("POOL", bufs["woCS"][slice(0, 128), slice(0, 2 * DM)], bufs["woC"].full, name="lwoC")
    for (dst, src, w) in (("WzcS", "Wzc", 512), ("WxbaS", "Wxba", 516)):
        sv = View(bufs[src], 0, 1 << 30, 0, 1 << 30,
                  ap=bufs[src].h.rearrange("(dc p) m -> p dc m", p=128))
        dv = View(bufs[dst], 0, 128, 0, 16 * w,
                  ap=bufs[dst].h.rearrange("p (dc m) -> p dc m", dc=16))
        p.dma("POOL", dv, sv, name="lC" + dst)
    c.memset(bufs["SSf"].full, 0.0)
    c.memset(bufs["SSb"].full, 0.0)
    for blk in range(T // 512):
        t0 = blk * 512
        rnk, q0 = blk // 2, (blk % 2) * 512
        xsrc = View(bufs["agout2"], 0, 1 << 30, 0, 1 << 30,
                    ap=bufs["agout2"].h[rnk * DM:(rnk + 1) * DM, q0:q0 + 512]
                    .rearrange("(dc p) q -> p dc q", p=128))
        xdst = View(bufs["xbfS"], 0, 128, 0, 16 * 512,
                    ap=bufs["xbfS"].h.rearrange("p (dc t) -> p dc t", dc=16))
        p.dma("POOL", xdst, xsrc, name="ldx")
        for mt in range(4):
            for dc in range(16):
                c.mm(bufs["pc0"][:, 0:512], bufs["WzcS"][:, dc * 512 + mt * 128:dc * 512 + mt * 128 + 128],
                     bufs["xbfS"][:, dc * 512:(dc + 1) * 512],
                     start=(dc == 0), stop=(dc == 15), name="pzc")
            dstb = bufs["zT"] if mt < 2 else bufs["cT"]
            hp = mt % 2
            c.cp(View(dstb, 0, 128, hp * 512, hp * 512 + 512,
                      ap=dstb.h[:, hp * 512:hp * 512 + 512]),
                 bufs["pc0"][:, 0:512], name="zcev")
        for t4 in range(4):
            for half in range(2):
                for dc in range(16):
                    c.mm(bufs["pc1"][:, half * 256:half * 256 + 256],
                         bufs["xbfS"][:, dc * 512 + t4 * 128:dc * 512 + t4 * 128 + 128],
                         bufs["WxbaS"][:, dc * 516 + half * 256:dc * 516 + half * 256 + 256],
                         start=(dc == 0), stop=(dc == 15), name="pxb")
            for dc in range(16):
                c.mm(bufs["pc2"][:, 0:4],
                     bufs["xbfS"][:, dc * 512 + t4 * 128:dc * 512 + t4 * 128 + 128],
                     bufs["WxbaS"][:, dc * 516 + 512:dc * 516 + 516],
                     start=(dc == 0), stop=(dc == 15), name="pa")
            c.cp(View(bufs["xtok"], 0, 128, t4 * 256, t4 * 256 + 256,
                      ap=bufs["xtok"].h[:, t4 * 256:t4 * 256 + 256]), bufs["pc1"][:, 0:256], name="xtev")
            c.cp(View(bufs["btok"], 0, 128, t4 * 256, t4 * 256 + 256,
                      ap=bufs["btok"].h[:, t4 * 256:t4 * 256 + 256]), bufs["pc1"][:, 256:512], name="btev")
            c.cp(View(bufs["atok"], 0, 128, t4 * 4, t4 * 4 + 4,
                      ap=bufs["atok"].h[:, t4 * 4:t4 * 4 + 4]), bufs["pc2"][:, 0:4], name="atev")
        for t4 in range(4):
            ch = 4 * blk + t4
            for h in range(HPC):
                hp, hl = h // 2, h % 2
                hb = 64 * hl
                acol = bufs["atok"][slice(0, 128), slice(t4 * 4 + h, t4 * 4 + h + 1)]
                c.act(bufs["ldc"][:, 0:1], acol, AF.Exp, name="cEa")
                c.ts(bufs["ldc"][:, 0:1], bufs["ldc"][:, 0:1], -1.0, ALU.mult, name="cNeg")
                c.mm(bufs["pc2"][:, 0:1], _c(bufs, "cumIncl"), bufs["ldc"][:, 0:1], name="cWc")
                c.cp(bufs["wcol"][:, 0:1], bufs["pc2"][:, 0:1], name="cWcev")
                c.mm(bufs["pc2"][0:1, 4:132], bufs["ldc"][:, 0:1], _c(bufs, "cumIncl"), name="cWr")
                c.cp(bufs["wrow2"][0:1, 0:128], bufs["pc2"][0:1, 4:132], name="cWrev")
                c.ts(bufs["nwrow"][0:1, 0:128], bufs["wrow2"][0:1, 0:128], -1.0, ALU.mult, name="cNwr")
                c.mm(bufs["pc2"][0:1, 132:133], onescol, bufs["ldc"][:, 0:1], name="cTot")
                c.cp(bufs["tot"][0:1, 0:1], bufs["pc2"][0:1, 132:133], name="cTotev")
                c.act(bufs["ewl"][0:1, 0:1], bufs["tot"][0:1, 0:1], AF.Exp, name="cEwl")
                c.mm(bufs["pc2"][:, 136:137], onesr(128), bufs["ewl"][0:1, 0:1], name="cEwlb")
                c.cp(bufs["ewlb2"][:, 0:1], bufs["pc2"][:, 136:137], name="cEwlbev")
                c.mm(bufs["pc2"][:, 140:141], onesr(128), bufs["tot"][0:1, 0:1], name="cWlb")
                c.cp(bufs["wlb"][:, 0:1], bufs["pc2"][:, 140:141], name="cWlbev")
                p.op("ACT", lambda e, b=bufs: e.activation(b["ewsc"].h[:, 0:1], b["wcol"].h[:, 0:1],
                                                           AF.Exp, bias=b["wlb"].h[:, 0:1], scale=-1.0),
                     reads=[bufs["wcol"][:, 0:1], bufs["wlb"][:, 0:1]],
                     writes=[bufs["ewsc"][:, 0:1]], name="cEws")
                c.act(bufs["ewrow"][0:1, 0:128], bufs["wrow2"][0:1, 0:128], AF.Exp, name="cEwr")
                c.mm(bufs["pc0"][:, 0:128], bufs["nwrow"][0:1, 0:128], onesr(128),
                     start=True, stop=False, name="cD1")
                c.mm(bufs["pc0"][:, 0:128], onesr(128), bufs["wrow2"][0:1, 0:128],
                     start=False, stop=True, name="cD2")
                c.tt(bufs["Dxp"][:, 0:128], bufs["pc0"][:, 0:128], _c(bufs, "masknegS"), ALU.subtract, name="cDm")
                c.act(bufs["Dxp"][:, 0:128], bufs["Dxp"][:, 0:128], AF.Exp, name="cDe")
                btv = View(bufs["btok"], 0, 128, t4 * 256 + 64 * h, t4 * 256 + 64 * h + 64,
                           ap=bufs["btok"].h[:, t4 * 256 + 64 * h:t4 * 256 + 64 * h + 64])
                c.tr(bufs["pcb"][hb:hb + 64, 0:128], btv, bufs["identb"][slice(0, 128), slice(0, 128)], name="cBtr")
                c.cp(bufs["bTd"][hb:hb + 64, 0:128], bufs["pcb"][hb:hb + 64, 0:128], name="cBtrev")
                ctv = View(bufs["cT"], hb, hb + 64, hp * 512 + t4 * 128, hp * 512 + t4 * 128 + 128,
                           ap=bufs["cT"].h[hb:hb + 64, hp * 512 + t4 * 128:hp * 512 + t4 * 128 + 128])
                c.mm(bufs["pc1"][:, 128:256], bufs["bTd"][hb:hb + 64, 0:128], ctv, name="cCB")
                c.tt(bufs["Rsb"][:, 0:128], bufs["Dxp"][:, 0:128], bufs["pc1"][:, 128:256], ALU.mult, name="cR")
                c.mm(bufs["pc1"][:, 256:384], onesr(128), bufs["ewrow"][0:1, 0:128], name="cEbc")
                c.tt(bufs["csb"][hb:hb + 64, 0:128], ctv, bufs["pc1"][hb:hb + 64, 256:384], ALU.mult, name="cCs")
                xtv = View(bufs["xtok"], 0, 128, t4 * 256 + 64 * h, t4 * 256 + 64 * h + 64,
                           ap=bufs["xtok"].h[:, t4 * 256 + 64 * h:t4 * 256 + 64 * h + 64])
                c.mm(bufs["pcy"][hb:hb + 64, 0:128], xtv, bufs["Rsb"][:, 0:128], start=True, stop=False, name="cYi")
                ssb = View(bufs["SSb"], hb, hb + 64, hp * 64, hp * 64 + 64,
                           ap=bufs["SSb"].h[hb:hb + 64, hp * 64:hp * 64 + 64])
                c.mm(bufs["pcy"][hb:hb + 64, 0:128], ssb, bufs["csb"][hb:hb + 64, 0:128], start=False, stop=True, name="cYs")
                c.tr(bufs["pcb"][hb:hb + 64, 128:256], xtv, bufs["identb"][slice(0, 128), slice(0, 128)], name="cXtr")
                c.cp(bufs["xTd"][hb:hb + 64, 0:128], bufs["pcb"][hb:hb + 64, 128:256], name="cXtrev")
                ztv = View(bufs["zT"], hb, hb + 64, hp * 512 + t4 * 128, hp * 512 + t4 * 128 + 128,
                           ap=bufs["zT"].h[hb:hb + 64, hp * 512 + t4 * 128:hp * 512 + t4 * 128 + 128])
                c.act(bufs["silz"][hb:hb + 64, 0:128], ztv, AF.Silu, name="cSz")
                c.ts(bufs["t64"][hb:hb + 64, 0:128], bufs["xTd"][hb:hb + 64, 0:128],
                     bufs["DhS"][slice(hb, hb + 64), slice(hp, hp + 1)], ALU.mult, name="cDx")
                c.tt(bufs["t64"][hb:hb + 64, 0:128], bufs["t64"][hb:hb + 64, 0:128],
                     bufs["pcy"][hb:hb + 64, 0:128], ALU.add, name="cYa")
                yzv = View(bufs["yzT"], hb, hb + 64, hp * T + ch * 128, hp * T + ch * 128 + 128,
                           ap=bufs["yzT"].h[hb:hb + 64, hp * T + ch * 128:hp * T + ch * 128 + 128])
                p.op("DVE", lambda e, yzv=yzv, b=bufs, hb=hb: e.tensor_tensor(
                         yzv.ap, b["t64"].h[hb:hb + 64, 0:128],
                         b["silz"].h[hb:hb + 64, 0:128], ALU.mult),
                     reads=[bufs["t64"][hb:hb + 64, 0:128], bufs["silz"][hb:hb + 64, 0:128]],
                     writes=[yzv], name="cYz")
                c.ts(bufs["bsc"][:, 0:64], btv, bufs["ewsc"][:, 0:1], ALU.mult, name="cBsc")
                c.mm(bufs["pcy"][hb:hb + 64, 128:192], bufs["bsc"][:, 0:64], xtv, name="cSu")
                ssf = View(bufs["SSf"], hb, hb + 64, hp * 64, hp * 64 + 64,
                           ap=bufs["SSf"].h[hb:hb + 64, hp * 64:hp * 64 + 64])
                p.op("DVE", lambda e, ssf=ssf, b=bufs, hb=hb, hp=hp: e.scalar_tensor_tensor(
                         ssf.ap, ssf.ap, b["ewlb2"].h[hb:hb + 64, 0:1],
                         b["pcy"].h[hb:hb + 64, 128:192], ALU.mult, ALU.add),
                     reads=[ssf, bufs["ewlb2"][hb:hb + 64, 0:1], bufs["pcy"][hb:hb + 64, 128:192]],
                     writes=[ssf], name="cSup")
                c.cp(View(bufs["SSb"], hb, hb + 64, hp * 64, hp * 64 + 64,
                          ap=bufs["SSb"].h[hb:hb + 64, hp * 64:hp * 64 + 64]), ssf, name="cSbf")
    for tb in range(16):
        for hp in range(2):
            c.cp(bufs["scr0"][:, 0:512],
                 View(bufs["yzT"], 0, 128, hp * T + tb * 512, hp * T + tb * 512 + 512,
                      ap=bufs["yzT"].h[:, hp * T + tb * 512:hp * T + tb * 512 + 512]), name="yzf")
            c.act(bufs["scr1"][:, 0:512], bufs["scr0"][:, 0:512], AF.Square, name="yzsq")
            c.mm(bufs["pcs"][0:1, 0:512], onescol, bufs["scr0"][:, 0:512],
                 start=(hp == 0), stop=(hp == 1), name="stsum")
            c.mm(bufs["pc2"][0:1, 0:512], onescol, bufs["scr1"][:, 0:512],
                 start=(hp == 0), stop=(hp == 1), name="stssq")
            if hp == 1:
                c.cp(bufs["rowa"][0:1, 0:512], bufs["pcs"][0:1, 0:512], name="sumev")
                c.cp(bufs["rowb"][0:1, 0:512], bufs["pc2"][0:1, 0:512], name="ssqev")
                p.dma("SP", View(bufs["arin"], 0, 1 << 30, 0, 1 << 30,
                                 ap=bufs["arin"].h[0:1, tb * 512:tb * 512 + 512]),
                      bufs["rowa"][0:1, 0:512], name="sar1")
                p.dma("SP", View(bufs["arin"], 0, 1 << 30, 0, 1 << 30,
                                 ap=bufs["arin"].h[1:2, tb * 512:tb * 512 + 512]),
                      bufs["rowb"][0:1, 0:512], name="sar2")
    p.op("POOL", lambda e: e.collective_compute(
        "AllReduce", ALU.add, replica_groups=[list(range(NCORE))],
        ins=[bufs["arin"].h], outs=[bufs["arout"].h]),
        reads=[bufs["arin"].full], writes=[bufs["arout"].full], name="ARst", coll=True)
    for tb in range(16):
        p.dma("SP", bufs["rowa"][0:1, 0:512],
              View(bufs["arout"], 0, 1 << 30, 0, 1 << 30,
                   ap=bufs["arout"].h[0:1, tb * 512:tb * 512 + 512]), name="lar1")
        p.dma("SP", bufs["rowb"][0:1, 0:512],
              View(bufs["arout"], 0, 1 << 30, 0, 1 << 30,
                   ap=bufs["arout"].h[1:2, tb * 512:tb * 512 + 512]), name="lar2")
        c.ts(bufs["rowa"][0:1, 0:512], bufs["rowa"][0:1, 0:512], 1.0 / DM, ALU.mult, name="cmu")
        c.tt(bufs["rowc"][0:1, 0:512], bufs["rowa"][0:1, 0:512], bufs["rowa"][0:1, 0:512], ALU.mult, name="cmu2")
        c.stt(bufs["rowc"][0:1, 0:512], bufs["rowb"][0:1, 0:512], 1.0 / DM, bufs["rowc"][0:1, 0:512],
              ALU.mult, ALU.subtract, name="cvar")
        c.act(bufs["rowc"][0:1, 0:512], bufs["rowc"][0:1, 0:512], AF.Sqrt,
              bias=_c(bufs, "c1e5", slice(0, 1)), name="csd")
        c.recip(bufs["rowc"][0:1, 0:512], bufs["rowc"][0:1, 0:512], name="crs")
        for hp in range(2):
            c.mm(bufs["pc0"][:, 0:512], onesr(128), bufs["rowa"][0:1, 0:512], name="nbmu")
            c.mm(bufs["pc1"][:, 0:512], onesr(128), bufs["rowc"][0:1, 0:512], name="nbrs")
            yv = View(bufs["yzT"], 0, 128, hp * T + tb * 512, hp * T + tb * 512 + 512,
                      ap=bufs["yzT"].h[:, hp * T + tb * 512:hp * T + tb * 512 + 512])
            c.cp(bufs["scr0"][:, 0:512], yv, name="nf")
            c.tt(bufs["scr0"][:, 0:512], bufs["scr0"][:, 0:512], bufs["pc0"][:, 0:512], ALU.subtract, name="n1")
            c.tt(bufs["scr0"][:, 0:512], bufs["scr0"][:, 0:512], bufs["pc1"][:, 0:512], ALU.mult, name="n2")
            c.ts(yv, bufs["scr0"][:, 0:512], bufs["gbCS"][slice(0, 128), slice(hp, hp + 1)],
                 ALU.mult, bufs["gbCS"][slice(0, 128), slice(2 + hp, 2 + hp + 1)], ALU.add, name="n3")
    for tt in range(T // 128):
        for ec in range(4):
            for h in range(HPC):
                hp, hl = h // 2, h % 2
                c.mm(bufs["pcs"][:, 0:512],
                     View(bufs["yzT"], 64 * hl, 64 * hl + 64, hp * T + tt * 128, hp * T + tt * 128 + 128,
                          ap=bufs["yzT"].h[64 * hl:64 * hl + 64, hp * T + tt * 128:hp * T + tt * 128 + 128]),
                     View(bufs["woCS"], 64 * hl, 64 * hl + 64, hp * DM + ec * 512, hp * DM + ec * 512 + 512,
                          ap=bufs["woCS"].h[64 * hl:64 * hl + 64, hp * DM + ec * 512:hp * DM + ec * 512 + 512]),
                     start=(h == 0), stop=(h == 3), name="wcpp")
            c.cp(View(bufs["wrowC"], 0, 128, ec * 512, ec * 512 + 512,
                      ap=bufs["wrowC"].h[:, ec * 512:ec * 512 + 512]), bufs["pcs"][:, 0:512], name="wcev")
        p.dma("SP", View(bufs["rs2in"], 0, 1 << 30, 0, 1 << 30,
                         ap=bufs["rs2in"].h[tt * 128:tt * 128 + 128, 0:DM]),
              bufs["wrowC"][slice(0, 128), slice(0, DM)], name="wcdma")
    p.barrier("endC")
    for cm in reversed(pstackC):
        cm.__exit__(None, None, None)
    for cm in reversed(stackC):
        cm.__exit__(None, None, None)


def emit_D_fused(c, bufs, nc, p):
    NT = TPC
    onesr = lambda w: bufs["cstS"][0:1, CST["ones"][0]:CST["ones"][0] + w]
    onescol = bufs["cstS"][slice(0, 128), CST["ones"][0]:CST["ones"][0] + 1]
    xch = lambda buf, ic, w=NT: bufs[buf][slice(0, 128), slice(ic * w, ic * w + w)]
    stack = []
    mk = lambda nm, shape, dt, sp='S': bufs.__setitem__(nm, mkbuf(p, nc, stack, "sD_" + nm, shape, dt, sp))
    mk("lnsq", (128, NT), F32)
    for nm in ("lnsum", "lnssq", "lnmu", "lnvar", "lnt", "lnrs", "sumsqx"):
        mk(nm, (1, NT), F32)
    mk("lnmub", (128, NT), F32)
    mk("lnrsb", (128, NT), F32)
    pstack = []
    for nm, shape in (("pp3", (128, 512)), ("ppw", (128, 1024)),
                      ("ppg", (128, 1024)), ("ppu", (128, 1024))):
        bufs[nm] = mkbuf(p, nc, pstack, "pD_" + nm, shape, F32, 'P')
    bufs["pp0"] = bufs["pp3"]
    bufs["pbtD"] = mkbuf(p, nc, pstack, "pD_pbt", (128, 128), BF16, 'P')
    stackA = []
    mkA = lambda nm, shape, dt, sp='S': bufs.__setitem__(nm, mkbuf(p, nc, stackA, "sD_" + nm, shape, dt, sp))
    mkA("xT", (128, 16 * NT), F32)
    mkA("attnb", (128, DM), BF16)
    xsv = View(bufs["x2sav"], 0, 1 << 30, 0, 1 << 30,
               ap=bufs["x2sav"].h.rearrange("(dc p) m -> p dc m", p=128))
    p.dma("SP", View(bufs["xT"], 0, 128, 0, 16 * NT,
                     ap=bufs["xT"].h.rearrange("p (dc m) -> p dc m", dc=16)), xsv, name="lxD")
    for tt8 in range(8):
        sv = View(bufs["rs2out"], 0, 1 << 30, 0, 1 << 30,
                  ap=bufs["rs2out"].h[tt8 * 128:(tt8 + 1) * 128, 0:DM])
        p.dma("SP", bufs["attnb"][slice(0, 128), slice(0, DM)], sv, name="latnD")
        for ec in range(16):
            c.tr(bufs["pbtD"][slice(0, 128), slice(0, 128)],
                 bufs["attnb"][slice(0, 128), slice(ec * 128, ec * 128 + 128)],
                 bufs["identb"][slice(0, 128), slice(0, 128)], name="atrD")
            c.cp(bufs["lnsq"][:, 0:128], bufs["pbtD"][slice(0, 128), slice(0, 128)], name="attfD")
            xtv = View(bufs["xT"], 0, 128, ec * NT + tt8 * 128, ec * NT + tt8 * 128 + 128,
                       ap=bufs["xT"].h[:, ec * NT + tt8 * 128:ec * NT + tt8 * 128 + 128])
            c.tt(xtv, xtv, bufs["lnsq"][:, 0:128], ALU.add, name="woresD")
    _ln_emit(c, bufs, [xch("xT", ic) for ic in range(16)],
             [xch("xT", ic) for ic in range(16)], None, 16, NT, 1e-5, "l3f")
    for ic in range(16):
        p.dma("SP", View(bufs["x3sav"], 0, 1 << 30, 0, 1 << 30,
                         ap=bufs["x3sav"].h[ic * 128:(ic + 1) * 128, 0:NT]),
              xch("xT", ic), name="sx3F")
        p.dma("POOL", View(bufs["x3bsav"], 0, 1 << 30, 0, 1 << 30,
                           ap=bufs["x3bsav"].h[ic * 128:(ic + 1) * 128, 0:NT]),
              xch("xT", ic), name="sx3bF")
    p.barrier("p1endF")
    for cm in reversed(stackA):
        cm.__exit__(None, None, None)
    mk("x3b", (128, 16 * NT), BF16)
    mk("accS", (128, 16 * NT), F32)
    mk("hS", (128, 6 * NT), BF16)
    mk("wgS", (128, 2 * 16 * 128), BF16)
    mk("wuS", (128, 2 * 16 * 128), BF16)
    mk("w2S", (128, 6 * 16 * 128), BF16)
    mk("rmx", (128, 1), F32)
    mk("rsum", (128, 1), F32)
    mk("sigS", (128, 8), F32)
    bufs["crow"] = bufs["lnsum"]
    bufs["cbc"] = bufs["lnmub"]
    mk("silS", (128, NT), F32)
    x3b3 = bufs["x3bsav"].h.rearrange("(dc p) t -> p dc t", p=128)
    p.dma("SP", View(bufs["x3b"], 0, 128, 0, 16 * NT,
                     ap=bufs["x3b"].h.rearrange("p (dc t) -> p dc t", dc=16)),
          View(bufs["x3bsav"], 0, 1 << 30, 0, 1 << 30, ap=x3b3), name="lx3b")
    gt3 = bufs["gateT"].h.rearrange("(dc p) m -> p dc m", p=128)
    mk("gwS", (128, 16 * 17), BF16)
    gsrc = np.zeros(0)
    p.dma("POOL", View(bufs["gwS"], 0, 128, 0, 16 * 17,
                       ap=bufs["gwS"].h.rearrange("p (dc m) -> p dc m", dc=16)),
          View(bufs["gateT"], 0, 1 << 30, 0, 1 << 30,
               ap=bufs["gateT"].h.rearrange("(dc p) m -> p dc m", p=128)), name="lgw")
    for tt8 in range(8):
        for dc in range(16):
            c.mm(bufs["pp0"][:, 0:17], bufs["x3b"][:, dc * NT + tt8 * 128:dc * NT + tt8 * 128 + 128],
                 bufs["gwS"][:, dc * 17:dc * 17 + 17], start=(dc == 0), stop=(dc == 15), name="rlog")
        rw = View(bufs["lnrsb"], 0, 128, tt8 * 16, tt8 * 16 + 16,
                  ap=bufs["lnrsb"].h[:, tt8 * 16:tt8 * 16 + 16])
        c.red(bufs["rmx"][:, 0:1], bufs["pp0"][:, 0:16], ALU.max, name="rmax")
        c.ts(bufs["rmx"][:, 0:1], bufs["rmx"][:, 0:1], -1.0, ALU.mult, name="rneg")
        p.op("ACT", lambda e, rw=rw, b=bufs: e.activation(rw.ap, b["pp0"].h[:, 0:16], AF.Exp,
                                                          bias=b["rmx"].h[:, 0:1]),
             reads=[bufs["pp0"][slice(0, 128), slice(0, 16)], bufs["rmx"][:, 0:1]],
             writes=[rw], name="rexp")
        c.red(bufs["rsum"][:, 0:1], rw, ALU.add, name="rsum")
        c.recip(bufs["rsum"][:, 0:1], bufs["rsum"][:, 0:1], name="rrec")
        c.ts(rw, rw, bufs["rsum"][:, 0:1], ALU.mult, name="rnorm")
        c.act(bufs["sigS"][:, tt8:tt8 + 1], bufs["pp0"][:, 16:17], AF.Sigmoid, name="rsig")
        t_ = bufs["lnsq"]
        c.cp(t_[:, 0:16], rw, name="tau0")
        for it in range(3):
            c.red(bufs["rmx"][:, 0:1], t_[:, 0:16], ALU.max, name="taum")
            c.ts(t_[:, 16:32], t_[:, 0:16], bufs["rmx"][:, 0:1], ALU.is_ge, name="taug")
            c.ts(t_[:, 16:32], t_[:, 16:32], 1e30, ALU.mult, name="taus")
            c.tt(t_[:, 0:16], t_[:, 0:16], t_[:, 16:32], ALU.subtract, name="tausb")
        c.red(bufs["rmx"][:, 0:1], t_[:, 0:16], ALU.max, name="tauf")
        c.ts(t_[:, 0:16], rw, bufs["rmx"][:, 0:1], ALU.is_ge, name="cof1")
        c.tt(t_[:, 0:16], t_[:, 0:16], rw, ALU.mult, name="cof2")
        for e_ in range(16):
            p.dma("SP", View(bufs["coefD"], 0, 1 << 30, 0, 1 << 30,
                             ap=bufs["coefD"].h[e_:e_ + 1, tt8 * 128:tt8 * 128 + 128]),
                  t_[:, e_:e_ + 1], name="scoef")
    c.memset(bufs["accS"].full, 0.0)
    sg3 = bufs["sguT"].h.rearrange("(dc p) m -> p dc m", p=128)
    alljt = list(range(44))
    for g0 in range(0, 44, 6):
        sub = alljt[g0:g0 + 6]
        for si, jg in enumerate(sub):
            p.dma("POOL", View(bufs["wgS"], 0, 128, (si % 2) * 2048, (si % 2) * 2048 + 2048,
                               ap=bufs["wgS"].h.rearrange("p (r dc m) -> p r dc m", r=2, dc=16)[:, si % 2]),
                  View(bufs["sguT"], 0, 1 << 30, 0, 1 << 30,
                       ap=sg3[:, :, jg * 128:jg * 128 + 128]), name="lsg")
            p.dma("POOL", View(bufs["wuS"], 0, 128, (si % 2) * 2048, (si % 2) * 2048 + 2048,
                               ap=bufs["wuS"].h.rearrange("p (r dc m) -> p r dc m", r=2, dc=16)[:, si % 2]),
                  View(bufs["sguT"], 0, 1 << 30, 0, 1 << 30,
                       ap=sg3[:, :, 5632 + jg * 128:5632 + jg * 128 + 128]), name="lsu")
            p.dma("POOL", View(bufs["w2S"], 0, 128, si * 2048, si * 2048 + 2048,
                               ap=bufs["w2S"].h.rearrange("p (j dc m) -> p j dc m", j=6, dc=16)[:, si]),
                  View(bufs["sdT"], 0, 1 << 30, 0, 1 << 30,
                       ap=bufs["sdT"].h.rearrange("(jc p) m -> p jc m", p=128)[:, jg]), name="lsd")
            for th in range(2):
                for dc in range(16):
                    c.mm(bufs["ppg"][:, th * 512:th * 512 + 512],
                         bufs["wgS"][:, (si % 2) * 2048 + dc * 128:(si % 2) * 2048 + dc * 128 + 128],
                         View(bufs["x3b"], 0, 128, dc * NT + th * 512, dc * NT + th * 512 + 512,
                              ap=bufs["x3b"].h[:, dc * NT + th * 512:dc * NT + th * 512 + 512]),
                         start=(dc == 0), stop=(dc == 15), name="sgmm")
                    c.mm(bufs["ppu"][:, th * 512:th * 512 + 512],
                         bufs["wuS"][:, (si % 2) * 2048 + dc * 128:(si % 2) * 2048 + dc * 128 + 128],
                         View(bufs["x3b"], 0, 128, dc * NT + th * 512, dc * NT + th * 512 + 512,
                              ap=bufs["x3b"].h[:, dc * NT + th * 512:dc * NT + th * 512 + 512]),
                         start=(dc == 0), stop=(dc == 15), name="summ")
            c.act(bufs["silS"][:, 0:NT], bufs["ppg"][:, 0:NT], AF.Silu, name="ssil")
            c.tt(View(bufs["hS"], 0, 128, si * NT, si * NT + NT,
                      ap=bufs["hS"].h[:, si * NT:si * NT + NT]),
                 bufs["silS"][:, 0:NT], bufs["ppu"][:, 0:NT], ALU.mult, name="sh")
        for ec in range(16):
            for th in range(2):
                for si in range(len(sub)):
                    c.mm(bufs["ppw"][:, 0:512],
                         bufs["w2S"][:, si * 2048 + ec * 128:si * 2048 + ec * 128 + 128],
                         View(bufs["hS"], 0, 128, si * NT + th * 512, si * NT + th * 512 + 512,
                              ap=bufs["hS"].h[:, si * NT + th * 512:si * NT + th * 512 + 512]),
                         start=(si == 0), stop=(si == len(sub) - 1), name="sdmm")
                av = View(bufs["accS"], 0, 128, ec * NT + th * 512, ec * NT + th * 512 + 512,
                          ap=bufs["accS"].h[:, ec * NT + th * 512:ec * NT + th * 512 + 512])
                c.tt(av, av, bufs["ppw"][:, 0:512], ALU.add, name="sacc")
    for tt8 in range(8):
        c.tr(bufs["pp3"][0:1, 0:128], bufs["sigS"][:, tt8:tt8 + 1], _c(bufs, "ident"), name="sgtr")
        c.cp(View(bufs["crow"], 0, 1, tt8 * 128, tt8 * 128 + 128,
                  ap=bufs["crow"].h[0:1, tt8 * 128:tt8 * 128 + 128]), bufs["pp3"][0:1, 0:128], name="sgtev")
    for th in range(2):
        c.mm(bufs["ppw"][:, th * 512:th * 512 + 512], onesr(128),
             bufs["crow"][0:1, th * 512:th * 512 + 512], name="sgbc")
    c.cp(bufs["cbc"][:, 0:NT], bufs["ppw"][:, 0:NT], name="sgbcev")
    for ic in range(16):
        c.tt(xch("accS", ic), xch("accS", ic), bufs["cbc"][:, 0:NT], ALU.mult, name="sgmul")
    w13r = bufs["w13T"].h.rearrange("(e dc p) m -> p e dc m", e=16, p=128)
    w2r = bufs["w2T"].h.rearrange("(e jc p) m -> p e jc m", e=16, p=128)
    for e_ in range(16):
        p.dma("SP", bufs["crow"][0:1, 0:NT],
              View(bufs["coefD"], 0, 1 << 30, 0, 1 << 30, ap=bufs["coefD"].h[e_:e_ + 1, 0:NT]), name="lcoef")
        for th in range(2):
            c.mm(bufs["ppw"][:, th * 512:th * 512 + 512], onesr(128),
                 bufs["crow"][0:1, th * 512:th * 512 + 512], name="cbcmm")
        c.cp(bufs["cbc"][:, 0:NT], bufs["ppw"][:, 0:NT], name="cbcev")
        for g0 in range(0, 11, 6):
            sub = list(range(11))[g0:g0 + 6]
            for si, jt in enumerate(sub):
                p.dma("POOL", View(bufs["wgS"], 0, 128, (si % 2) * 2048, (si % 2) * 2048 + 2048,
                                   ap=bufs["wgS"].h.rearrange("p (r dc m) -> p r dc m", r=2, dc=16)[:, si % 2]),
                      View(bufs["w13T"], 0, 1 << 30, 0, 1 << 30,
                           ap=w13r[:, e_, :, jt * 128:jt * 128 + 128]), name="lg13")
                p.dma("POOL", View(bufs["wuS"], 0, 128, (si % 2) * 2048, (si % 2) * 2048 + 2048,
                                   ap=bufs["wuS"].h.rearrange("p (r dc m) -> p r dc m", r=2, dc=16)[:, si % 2]),
                      View(bufs["w13T"], 0, 1 << 30, 0, 1 << 30,
                           ap=w13r[:, e_, :, 1408 + jt * 128:1408 + jt * 128 + 128]), name="lu13")
                p.dma("POOL", View(bufs["w2S"], 0, 128, si * 2048, si * 2048 + 2048,
                                   ap=bufs["w2S"].h.rearrange("p (j dc m) -> p j dc m", j=6, dc=16)[:, si]),
                      View(bufs["w2T"], 0, 1 << 30, 0, 1 << 30, ap=w2r[:, e_, jt]), name="lw2e")
                for th in range(2):
                    for dc in range(16):
                        c.mm(bufs["ppg"][:, th * 512:th * 512 + 512],
                             bufs["wgS"][:, (si % 2) * 2048 + dc * 128:(si % 2) * 2048 + dc * 128 + 128],
                             View(bufs["x3b"], 0, 128, dc * NT + th * 512, dc * NT + th * 512 + 512,
                                  ap=bufs["x3b"].h[:, dc * NT + th * 512:dc * NT + th * 512 + 512]),
                             start=(dc == 0), stop=(dc == 15), name="egmm")
                        c.mm(bufs["ppu"][:, th * 512:th * 512 + 512],
                             bufs["wuS"][:, (si % 2) * 2048 + dc * 128:(si % 2) * 2048 + dc * 128 + 128],
                             View(bufs["x3b"], 0, 128, dc * NT + th * 512, dc * NT + th * 512 + 512,
                                  ap=bufs["x3b"].h[:, dc * NT + th * 512:dc * NT + th * 512 + 512]),
                             start=(dc == 0), stop=(dc == 15), name="eumm")
                c.act(bufs["silS"][:, 0:NT], bufs["ppg"][:, 0:NT], AF.Silu, name="esil")
                c.tt(bufs["silS"][:, 0:NT], bufs["silS"][:, 0:NT], bufs["ppu"][:, 0:NT], ALU.mult, name="ehu")
                c.tt(View(bufs["hS"], 0, 128, si * NT, si * NT + NT,
                          ap=bufs["hS"].h[:, si * NT:si * NT + NT]),
                     bufs["silS"][:, 0:NT], bufs["cbc"][:, 0:NT], ALU.mult, name="ehc")
            for ec in range(16):
                for th in range(2):
                    for si in range(len(sub)):
                        c.mm(bufs["ppw"][:, 0:512],
                             bufs["w2S"][:, si * 2048 + ec * 128:si * 2048 + ec * 128 + 128],
                             View(bufs["hS"], 0, 128, si * NT + th * 512, si * NT + th * 512 + 512,
                                  ap=bufs["hS"].h[:, si * NT + th * 512:si * NT + th * 512 + 512]),
                             start=(si == 0), stop=(si == len(sub) - 1), name="edmm")
                    av = View(bufs["accS"], 0, 128, ec * NT + th * 512, ec * NT + th * 512 + 512,
                              ap=bufs["accS"].h[:, ec * NT + th * 512:ec * NT + th * 512 + 512])
                    c.tt(av, av, bufs["ppw"][:, 0:512], ALU.add, name="eacc")
    for ic in range(16):
        p.dma("SP", bufs["lnmub"][:, 0:NT],
              View(bufs["x3sav"], 0, 1 << 30, 0, 1 << 30,
                   ap=bufs["x3sav"].h[ic * 128:(ic + 1) * 128, 0:NT]), name="lx3f")
        c.tt(xch("accS", ic), xch("accS", ic), bufs["lnmub"][:, 0:NT], ALU.add, name="fres")
    _ln_emit(c, bufs, [xch("accS", ic) for ic in range(16)],
             [xch("accS", ic) for ic in range(16)], None, 16, NT, 1e-5, "lf")
    for ic in range(16):
        p.dma("SP", View(bufs["x4T"], 0, 1 << 30, 0, 1 << 30,
                         ap=bufs["x4T"].h[ic * 128:(ic + 1) * 128, 0:NT]),
              xch("accS", ic), name="sx4")
    for cm in reversed(pstack):
        cm.__exit__(None, None, None)
    for cm in reversed(stack):
        cm.__exit__(None, None, None)



def build_full(nc):
    NT = TPC
    p = Prog(nc)
    c = Ctx(p)
    gstack = []
    bufs = {}
    mkG = lambda nm, shape, dt, sp='S': bufs.__setitem__(nm, mkbuf(p, nc, gstack, "s_" + nm, shape, dt, sp))

    dd = lambda nm, shape, dt, kind="ExternalInput", asp="Local": bufs.__setitem__(
        nm, dbuf(nc, nm, shape, dt, kind=kind, addr_space=asp))
    dd("xT", (DM, T), F32)
    for g in range(2):
        dd(f"wprojD{g}", (DM, 288), F32)
        bufs[f"wprojD{g}"] = bufs[f"wprojD{g}"]
        dd(f"wvD{g}", (DM, 132), F32)
        dd(f"w2cD{g}", (32, 128), F32)
        dd(f"convD{g}", (128, 4), F32)
        dd(f"wvbD{g}", (1, 132), F32)
        dd(f"woAD{g}", (128, DM), F32)
    dd("cstD", (128, CST_W), F32)
    dd("qknD", (128, 2), F32)
    dd("onesbfD", (3, T), BF16)
    dd("identbD", (128, 128), BF16)
    dd("x0T", (DM, NT), F32)
    dd("swA", (DM, 64), F32)
    dd("mdT", (DM, 8), F32)
    dd("updA", (DM, 64), F32)
    dd("dgnA", (64, DM), F32)
    dd("biasW", (8, DM), F32)
    dd("dbT", (DM, 8), F32)
    dd("impc", (8, 2), F32)
    dd("Wzc", (DM, 512), F32)
    dd("Wxba", (DM, 516), F32)
    dd("Dh", (128, 2), F32)
    dd("gbC", (128, 4), F32)
    dd("woC", (128, 2 * DM), F32)
    dd("gateT", (DM, 17), F32)
    dd("sguT", (DM, 11264), F32)
    dd("sdT", (5632, DM), F32)
    dd("w13T", (16 * DM, 2816), F32)
    dd("w2T", (16 * 1408, DM), F32)
    dd("x4T", (DM, NT), F32, "ExternalOutput")
    dd("rs1in", (T, DM), BF16, "Internal")
    dd("rs1out", (NT, DM), BF16, "Internal")
    dd("agin2", (DM, NT), BF16, "Internal")
    dd("agout2", (NCORE * DM, NT), BF16, "Internal", "Shared")
    dd("arin", (2, T), F32, "Internal")
    dd("arout", (2, T), F32, "Internal")
    dd("rs2in", (T, DM), BF16, "Internal")
    dd("rs2out", (NT, DM), BF16, "Internal")
    dd("x2sav", (DM, NT), F32, "Internal")
    dd("x3sav", (DM, NT), F32, "Internal")
    dd("x3bsav", (DM, NT), BF16, "Internal")
    dd("coefD", (16, NT), F32, "Internal")

    mkG("cstS", (128, CST_W), F32)
    mkG("identb", (128, 128), BF16)
    p.dma("SP", bufs["cstS"].full, bufs["cstD"].full, name="lcst")
    p.dma("SP", bufs["identb"].full, bufs["identbD"].full, name="lidb")
    onesr = lambda w: bufs["cstS"][0:1, CST["ones"][0]:CST["ones"][0] + w]
    onescol = bufs["cstS"][slice(0, 128), CST["ones"][0]:CST["ones"][0] + 1]
    xch = lambda buf, ic, w=NT: bufs[buf][slice(0, 128), slice(ic * w, ic * w + w)]

    stackA = []
    mkA = lambda nm, shape, dt, sp='S': bufs.__setitem__(nm, mkbuf(p, nc, stackA, "sA_" + nm, shape, dt, sp))
    mkA("qknS", (128, 2), F32)
    mkA("onesbf", (3, T), BF16)
    p.dma("SP", bufs["qknS"].full, bufs["qknD"].full, name="lqkn")
    p.dma("POOL", bufs["onesbf"].full, bufs["onesbfD"].full, name="lonesbf")
    mkA("wprojS", (128, 16 * 288), BF16)
    mkA("wvS", (128, 16 * 132), BF16)
    mkA("w2cS", (32, 128), F32)
    mkA("convS", (128, 4), F32)
    mkA("wvbS", (1, 132), F32)
    mkA("xbfS", (128, 16 * CO), BF16)
    for h in range(2):
        mkA(f"KC{h}", (70, T), BF16)
        mkA(f"Vx{h}", (128, (T // 128) * 65), BF16)
        mkA(f"Qext{h}", (70, 512), BF16)
        mkA(f"ploc{h}", (128, 4 * CO), BF16)
        mkA(f"Grw{h}", (1, CO), F32)
        mkA(f"Mrow{h}", (1, CO), F32)
        mkA(f"QnoM{h}", (70, 512), BF16)
        mkA(f"sloc{h}", (128, 2048), F32)
        mkA(f"CmT{h}", (64, 64), F32)
        mkA(f"CmTb{h}", (64, 64), BF16)
        mkA(f"onrm{h}", (64, CO), BF16)
        mkA(f"woA{h}", (64, DM), BF16)
    mkA("Mbc0", (128, CO), F32)
    bufs["Mbc1"] = bufs["Mbc0"]
    mkA("localS", (128, 512), F32)
    mkA("sCmS", (128, 64), F32)
    mkA("sGTS", (128, 64), F32)
    mkA("mxq", (128, 4), F32)
    mkA("r128a", (128, 1), F32)
    mkA("qnT", (128, CO), F32)
    mkA("knT", (128, CO), F32)
    mkA("wnT", (128, CO), F32)
    mkA("rawT", (128, CO), F32)
    mkA("sqT", (128, CO), F32)
    mkA("wrw", (128, CO + 3), F32)
    mkA("wcT", (128, CO), F32)
    mkA("xwsb", (32, CO), F32)
    mkA("betaS", (128, 8), F32)
    mkA("nbetaS", (128, 8), F32)
    mkA("gS", (128, 8), F32)
    mkA("GnS", (128, 8), F32)
    mkA("GtS", (128, 8), F32)
    mkA("sgS", (128, 2), F32)
    mkA("zaS", (128, 8), F32)
    mkA("carryS", (1, 2), F32)
    mkA("r512a", (1, CO), F32)
    mkA("r512b", (1, CO), F32)
    mkA("t2a", (1, CO), F32)
    bufs["zrow"] = bufs["t2a"]
    mkA("t2b", (1, CO), F32)
    mkA("stageG", (65, 1536), BF16)
    mkA("stageM", (65, 1536), BF16)
    mkA("ptile", (128, CO), BF16)
    mkA("ozsb", (65, CO), F32)
    mkA("zrec", (1, CO), F32)
    mkA("wrow", (128, DM), BF16)
    for nm in ("sWW", "sbd", "sO", "sLbd", "sdb", "sLbdT", "sL2", "sL2T",
               "sL4", "sL4T", "sL8", "sP", "sPT", "sN", "sRb", "ssWK",
               "ssKb", "ssKn", "sD", "sDb", "st"):
        mkA(nm, (128, 128), F32)
    for nm in ("sX", "sB", "sY"):
        mkA(nm, (128, 256), F32)
    mkA("sWtok", (128, 64), F32)
    mkA("sWbb", (128, 64), F32)
    mkA("sV", (128, 64), F32)
    mkA("sQt", (128, 128), F32)
    pstack = []
    mkP = lambda nm, shape: bufs.__setitem__(nm, mkbuf(p, nc, pstack, "p_" + nm, shape, F32, 'P'))
    for nm in ("pp0", "pp1", "pp2", "pp3", "ppi", "ppj"):
        mkP(nm, (128, 512))
    mkP("poz0", (65, 512))
    mkP("poz1", (65, 512))
    bufs["ppw2"] = bufs["ppi"]

    for g in range(2):
        for (dst, src, w) in (("wprojS", f"wprojD{g}", 288), ("wvS", f"wvD{g}", 132)):
            sv = View(bufs[src], 0, 1 << 30, 0, 1 << 30,
                      ap=bufs[src].h.rearrange("(dc p) m -> p dc m", p=128))
            dv = View(bufs[dst], 0, 128, 0, 16 * w,
                      ap=bufs[dst].h.rearrange("p (dc m) -> p dc m", dc=16))
            p.dma("POOL", dv, sv, name="lw" + dst)
        p.dma("SP", bufs["w2cS"].full, bufs[f"w2cD{g}"].full, name="lw2c")
        p.dma("SP", bufs["convS"].full, bufs[f"convD{g}"].full, name="lconv")
        p.dma("SP", bufs["wvbS"].full, bufs[f"wvbD{g}"].full, name="lwvb")
        for h in range(2):
            p.dma("POOL", bufs[f"woA{h}"][slice(0, 64), slice(0, DM)],
                  View(bufs[f"woAD{g}"], 0, 1 << 30, 0, 1 << 30,
                       ap=bufs[f"woAD{g}"].h[64 * h:64 * h + 64, 0:DM]), name="lwoA")
        emit_phaseA_group(c, bufs, T, g, fused=True)
        p.barrier(f"grpA{g}")
    for cm in reversed(stackA):
        cm.__exit__(None, None, None)
    p.op("POOL", lambda e: e.collective_compute(
        "ReduceScatter", ALU.add, replica_groups=[list(range(NCORE))],
        ins=[bufs["rs1in"].h], outs=[bufs["rs1out"].h]),
        reads=[bufs["rs1in"].full], writes=[bufs["rs1out"].full], name="RS1", coll=True)

    stackB = []
    mkB = lambda nm, shape, dt, sp='S': bufs.__setitem__(nm, mkbuf(p, nc, stackB, "sB_" + nm, shape, dt, sp))
    mkB("xT", (128, 16 * NT), F32)
    mkB("attnb", (128, DM), BF16)
    mkB("x1n", (128, 16 * NT), F32)
    mkB("swS", (128, 16 * 64), F32)
    mkB("updS", (128, 16 * 64), F32)
    mkB("dgnS", (64, 16 * 128), BF16)
    mkB("biasS", (8, 16 * 128), BF16)
    mkB("mdS", (128, 16 * 8), F32)
    mkB("dbS", (128, 16 * 8), F32)
    mkB("impS", (8, 2), F32)
    mkB("lnsq", (128, NT), F32)
    for nm in ("lnsum", "lnssq", "lnmu", "lnvar", "lnt", "lnrs", "sumsqx"):
        mkB(nm, (1, NT), F32)
    mkB("lnmub", (128, NT), F32)
    mkB("lnrsb", (128, NT), F32)
    mkB("cswS", (64, 8), F32)
    mkB("cswC", (64, 1), F32)
    mkB("udbC", (64, 1), F32)
    mkB("md2S", (8, 8), F32)
    mkB("md2C", (8, 1), F32)
    mkB("r2S", (8, NT), F32)
    mkB("gTt", (128, 8), F32)
    mkB("selc", (128, 8), F32)
    mkB("indt", (128, 8), F32)
    mkB("indS", (8, NT), BF16)
    mkB("mxiS", (64, NT), BF16)
    bufs["ppw"] = Buf(None, "dummy", (0, 0), 'S')
    for cm in reversed(pstack):
        cm.__exit__(None, None, None)
    pstack = []
    for nm in ("pp0", "ppq", "pp3"):
        mkP2 = mkbuf(p, nc, pstack, "pB_" + nm, (128, 512), F32, 'P')
        bufs[nm] = mkP2
    bufs["ppw"] = mkbuf(p, nc, pstack, "pB_ppw", (128, 1024), F32, 'P')
    bufs["pps"] = mkbuf(p, nc, pstack, "pB_pps", (64, 1024), F32, 'P')
    bufs["pbt"] = mkbuf(p, nc, pstack, "pB_pbt", (128, 128), BF16, 'P')
    xsv = View(bufs["x0T"], 0, 1 << 30, 0, 1 << 30,
               ap=bufs["x0T"].h.rearrange("(dc p) m -> p dc m", p=128))
    p.dma("SP", View(bufs["xT"], 0, 128, 0, 16 * NT,
                     ap=bufs["xT"].h.rearrange("p (dc m) -> p dc m", dc=16)), xsv, name="lxB")
    for tt8 in range(8):
        sv = View(bufs["rs1out"], 0, 1 << 30, 0, 1 << 30,
                  ap=bufs["rs1out"].h[tt8 * 128:(tt8 + 1) * 128, 0:DM])
        p.dma("SP", bufs["attnb"][slice(0, 128), slice(0, DM)], sv, name="latn")
        for ec in range(16):
            c.tr(bufs["pbt"][slice(0, 128), slice(0, 128)],
                 bufs["attnb"][slice(0, 128), slice(ec * 128, ec * 128 + 128)],
                 bufs["identb"][slice(0, 128), slice(0, 128)], name="atr")
            c.cp(bufs["lnsq"][:, 0:128], bufs["pbt"][slice(0, 128), slice(0, 128)], name="attf")
            xtv = View(bufs["xT"], 0, 128, ec * NT + tt8 * 128, ec * NT + tt8 * 128 + 128,
                       ap=bufs["xT"].h[:, ec * NT + tt8 * 128:ec * NT + tt8 * 128 + 128])
            c.tt(xtv, xtv, bufs["lnsq"][:, 0:128], ALU.add, name="woresB")
    _ln_emit(c, bufs, [xch("xT", ic) for ic in range(16)],
             [xch("x1n", ic) for ic in range(16)], None, 16, NT, 1e-5, "ln1")
    for (dst, src, w) in (("swS", "swA", 64), ("updS", "updA", 64),
                          ("mdS", "mdT", 8), ("dbS", "dbT", 8)):
        sv = View(bufs[src], 0, 1 << 30, 0, 1 << 30,
                  ap=bufs[src].h.rearrange("(dc p) m -> p dc m", p=128))
        dv = View(bufs[dst], 0, 128, 0, 16 * w,
                  ap=bufs[dst].h.rearrange("p (dc m) -> p dc m", dc=16))
        p.dma("POOL", dv, sv, name="lB" + dst)
    sv = View(bufs["dgnA"], 0, 1 << 30, 0, 1 << 30,
              ap=bufs["dgnA"].h.rearrange("n (dc m) -> n dc m", dc=16))
    p.dma("POOL", View(bufs["dgnS"], 0, 64, 0, 16 * 128,
                       ap=bufs["dgnS"].h.rearrange("n (dc m) -> n dc m", dc=16)), sv, name="ldgnB")
    sv = View(bufs["biasW"], 0, 1 << 30, 0, 1 << 30,
              ap=bufs["biasW"].h.rearrange("n (dc m) -> n dc m", dc=16))
    p.dma("POOL", View(bufs["biasS"], 0, 8, 0, 16 * 128,
                       ap=bufs["biasS"].h.rearrange("n (dc m) -> n dc m", dc=16)), sv, name="lbwB")
    p.dma("SP", bufs["impS"].full, bufs["impc"].full, name="limpB")
    emit_B_ffn(c, bufs, NT)
    _ln_emit(c, bufs, [xch("xT", ic) for ic in range(16)],
             [xch("x1n", ic) for ic in range(16)], None, 16, NT, 1e-5, "ln2")
    for ic in range(16):
        p.dma("SP", View(bufs["x2sav"], 0, 1 << 30, 0, 1 << 30,
                         ap=bufs["x2sav"].h[ic * 128:(ic + 1) * 128, 0:NT]),
              xch("x1n", ic), name="sx2f")
        p.dma("POOL", View(bufs["agin2"], 0, 1 << 30, 0, 1 << 30,
                           ap=bufs["agin2"].h[ic * 128:(ic + 1) * 128, 0:NT]),
              xch("x1n", ic), name="sx2b")
    p.barrier("endB")
    for cm in reversed(pstack):
        cm.__exit__(None, None, None)
    for cm in reversed(stackB):
        cm.__exit__(None, None, None)
    p.op("POOL", lambda e: e.collective_compute(
        "AllGather", ALU.bypass, replica_groups=[list(range(NCORE))],
        ins=[bufs["agin2"].h], outs=[bufs["agout2"].h]),
        reads=[bufs["agin2"].full], writes=[bufs["agout2"].full], name="AG2", coll=True)
    emit_C_fused(c, bufs, nc, p)
    p.op("POOL", lambda e: e.collective_compute(
        "ReduceScatter", ALU.add, replica_groups=[list(range(NCORE))],
        ins=[bufs["rs2in"].h], outs=[bufs["rs2out"].h]),
        reads=[bufs["rs2in"].full], writes=[bufs["rs2out"].full], name="RS2", coll=True)
    emit_D_fused(c, bufs, nc, p)
    p.barrier("fin")
    nops = p.emit()
    for cm in reversed(gstack):
        cm.__exit__(None, None, None)
    return nops


def prep_full_inputs(inputs, core):
    f = np.float32
    import ml_dtypes
    d = {}
    a = prep_A_inputs(inputs, core, T)
    d["xT"] = a["xT"]
    d["cstD"] = a["cst"]
    d["qknD"] = a["qkn"]
    d["onesbfD"] = a["onesbfD"]
    for g in range(2):
        d[f"wprojD{g}"] = a[f"wproj{g}"]
        d[f"wvD{g}"] = a[f"wv{g}"]
        d[f"w2cD{g}"] = a[f"w2c{g}"]
        d[f"convD{g}"] = a[f"conv{g}"]
        d[f"wvbD{g}"] = a[f"wvb{g}"]
    WoT = np.ascontiguousarray(np.asarray(inputs["L0_attn_Wo"], f).T)
    for g in range(2):
        wo = np.zeros((128, DM), f)
        for h in range(2):
            gh = 4 * core + 2 * g + h
            wo[64 * h:64 * h + 64] = WoT[64 * gh:64 * gh + 64]
        d[f"woAD{g}"] = wo
    b = prep_B_inputs(inputs, core, np.zeros((DM, T), f))
    for k in ("x0T", "swA", "mdT", "updA", "dgnA", "biasW", "dbT", "impc"):
        d[k] = b[k]
    d["identbD"] = np.eye(128, dtype=ml_dtypes.bfloat16)
    cc = prep_C_inputs(inputs, core, np.zeros((2, 2), f))
    for k in ("Wzc", "Wxba", "Dh"):
        d[k] = cc[k]
    h0 = HPC * core
    gamma = np.asarray(inputs["L1_attn_gamma"], f)
    beta = np.asarray(inputs["L1_attn_beta"], f)
    gbC = np.zeros((128, 4), f)
    WoutT = np.ascontiguousarray(np.asarray(inputs["L1_attn_W_out"], f).T)
    woC = np.zeros((128, 2 * DM), f)
    for hp in range(2):
        for hl in range(2):
            i0 = 64 * (h0 + 2 * hp + hl)
            gbC[64 * hl:64 * hl + 64, hp] = gamma[i0:i0 + 64]
            gbC[64 * hl:64 * hl + 64, 2 + hp] = beta[i0:i0 + 64]
            woC[64 * hl:64 * hl + 64, hp * DM:(hp + 1) * DM] = WoutT[i0:i0 + 64]
    d["gbC"] = gbC
    d["woC"] = woC
    dd_ = prep_D_inputs(inputs, core, np.zeros((2, 2), ml_dtypes.bfloat16), np.zeros((2, 2), f))
    for k in ("gateT", "sguT", "sdT", "w13T", "w2T"):
        d[k] = dd_[k]
    return d


FUSED = True


def kernel_fused(**inputs):
    ncF = _get("F", build_full)
    in_maps = [prep_full_inputs(inputs, core) for core in range(NCORE)]
    res = run_bass_kernel_spmd(ncF, in_maps, core_ids=list(range(NCORE)))
    x4 = np.concatenate([res.results[k]["x4T"] for k in range(NCORE)], axis=1)
    return np.ascontiguousarray(x4.T).reshape(1, T, DM).astype(np.float32)
